# version 02 (final)
# baseline (speedup 1.0000x reference)
import numpy as np
from contextlib import ExitStack

import concourse.bass as bass
import concourse.mybir as mybir
from concourse.bass_utils import run_bass_kernel_spmd

F32 = mybir.dt.float32
BF16 = mybir.dt.bfloat16
AF = mybir.ActivationFunctionType
ALU = mybir.AluOpType
AX = mybir.AxisListType

P = 128
D_MODEL = 1024
SEQ = 2048
NBATCH = 2
NT = SEQ // P
MAGIC = float(1.5 * 2 ** 23)
LN_EPS = 1e-5


class Sem:
    def __init__(self, nc, stack, name):
        self.h = stack.enter_context(nc.semaphore(name))
        self.v = 0

    def inc(self, inst, n=1):
        inst.then_inc(self.h, n)
        self.v += n
        return self.v


class Kb:

    def __init__(self, nc, stack):
        self.nc = nc
        self.stack = stack
        self.pb = [stack.enter_context(nc.psum_tensor(f"pb{i}", [P, 512], F32))
                   for i in range(6)]
        self.pt = None
        self._ptn = 0
        names = ["ld", "lg", "mm", "ev", "vx", "sx", "st", "gx"]
        self.s = {n: Sem(nc, stack, "sem_" + n) for n in names}
        self.ident = stack.enter_context(nc.sbuf_tensor("c_ident", [P, P], BF16))
        self.identf = stack.enter_context(nc.sbuf_tensor("c_identf", [P, P], F32))
        self.mask = stack.enter_context(nc.sbuf_tensor("c_mask", [P, P], F32))
        self.ones_col = stack.enter_context(nc.sbuf_tensor("c_ones", [1, P], F32))
        self.st12 = stack.enter_context(nc.sbuf_tensor("c_st12", [P, 48], F32))
        self.mv = stack.enter_context(nc.sbuf_tensor("c_mv", [P, 64], F32))
        self.colA = stack.enter_context(nc.sbuf_tensor("c_colA", [P, 32], F32))
        self.colB = stack.enter_context(nc.sbuf_tensor("c_colB", [P, 32], F32))
        self.colC = stack.enter_context(nc.sbuf_tensor("c_colC", [P, 32], F32))
        self.junk = stack.enter_context(nc.sbuf_tensor("c_junk", [P, 1024], BF16))
        self.qscr = stack.enter_context(nc.sbuf_tensor("c_qscr", [P, 256], F32))

    def pt_ctx(self, stack):
        self._ptn += 1
        self.pt = [stack.enter_context(
            self.nc.psum_tensor(f"pt{self._ptn}_{i}", [P, 1024], BF16))
            for i in range(2)]
        return self.pt

    def dma(self, eng, dst, src, sem="ld", slow=False, transpose=False):
        inst = eng.dma_start(dst, src, allow_slow_non_contiguous=slow,
                             transpose=transpose)
        return self.s[sem].inc(inst, 16)

    debug = False

    def debug_dump(self, name, ap):
        if not self.debug:
            return
        nc = self.nc
        dt = nc.dram_tensor("dbg_" + name, list(ap.shape), ap.dtype,
                            kind="ExternalOutput").ap()
        with nc.Block() as blk:
            @blk.sync
            def _(sy):
                self.dma(sy, dt, ap, sem="st")
                sy.wait_ge(self.s["st"].h, self.s["st"].v)


def bc_src(ap1d, n=P):
    return bass.AP(ap1d.tensor, ap1d.offset, [[0, n]] + [list(d) for d in ap1d.ap])


def emit_prelude(k: Kb):
    nc = k.nc
    with nc.Block() as blk:
        @blk.gpsimd
        def _(g):
            g.memset(k.ident[:], 1.0)
            g.affine_select(k.ident[:], k.ident[:], [[-1, P]], ALU.is_equal, 0.0,
                            base=0, channel_multiplier=1)
            g.memset(k.identf[:], 1.0)
            g.affine_select(k.identf[:], k.identf[:], [[-1, P]], ALU.is_equal, 0.0,
                            base=0, channel_multiplier=1)
            g.memset(k.mask[:], 0.0)
            g.affine_select(k.mask[:], k.mask[:], [[-1, P]], ALU.is_ge, -1e30,
                            base=0, channel_multiplier=1)
            g.memset(k.ones_col[:], 1.0)



def pipe(k: Kb, njobs, emit_mm, emit_ev, banks=(0, 1, 2, 3), ev_eng="scalar",
         extra=None):
    nc = k.nc
    mm, ev = k.s["mm"], k.s["ev"]
    mm0, ev0 = mm.v, ev.v
    bts = [k.pb[b] if isinstance(b, int) else b for b in banks]
    nb = len(bts)

    with nc.Block() as blk:
        @blk.tensor
        def _(t):
            for j in range(njobs):
                if j >= nb:
                    t.wait_ge(ev.h, ev0 + j - nb + 1)
                inst = emit_mm(t, j, bts[j % nb])
                inst.then_inc(mm.h, 1)

        def _ev(e):
            for j in range(njobs):
                e.wait_ge(mm.h, mm0 + j + 1)
                inst = emit_ev(e, j, bts[j % nb])
                inst.then_inc(ev.h, 1)

        getattr(blk, ev_eng)(_ev)
        if extra is not None:
            extra(blk)
    mm.v = mm0 + njobs
    ev.v = ev0 + njobs


def transpose_pipe(k: Kb, jobs, ident=None, ev_eng="vector", banks=None):
    ident = ident if ident is not None else k.ident[:]

    def emit_mm(t, j, pb):
        srcs, _ = jobs[j]
        for i, ssrc in enumerate(srcs):
            inst = t.transpose(pb[:, P * i:P * (i + 1)], ssrc, ident)
        return inst

    def emit_ev(e, j, pb):
        srcs, dst = jobs[j]
        nsrc = len(srcs)
        return e.tensor_copy(dst, pb[:, 0:P * nsrc].rearrange("p (c t) -> p c t", c=nsrc))

    banks = banks if banks is not None else [k.pt[0], k.pt[1]]
    pipe(k, len(jobs), emit_mm, emit_ev, banks=banks, ev_eng=ev_eng)


def bcast_row(k: Kb, src_row, dst, width):
    nch = (width + 511) // 512

    def emit_mm(t, j, pb):
        w = min(512, width - 512 * j)
        return t.matmul(pb[:, 0:w], k.ones_col[:], src_row[:, 512 * j:512 * j + w],
                        start=True, stop=True)

    def emit_ev(e, j, pb):
        w = min(512, width - 512 * j)
        return e.tensor_copy(dst[:, 512 * j:512 * j + w], pb[:, 0:w])

    pipe(k, nch, emit_mm, emit_ev, banks=(4, 5), ev_eng="vector")


def ln_stats(k: Kb, xviews, width, kind="ln", wait=None, junk=None):
    nc = k.nc
    n = len(xviews)
    assert n <= 32
    vx, sx = k.s["vx"], k.s["sx"]
    vx0, sx0 = vx.v, sx.v
    sums = k.mv[:, 0:n]
    sqs = k.mv[:, 32:32 + n]

    with nc.Block() as blk:
        @blk.vector
        def _(v):
            if wait is not None:
                v.wait_ge(k.s[wait[0]].h, wait[1])
            for i in range(n):
                jk = junk if junk is not None else k.junk[:, 0:width]
                if kind == "ln":
                    v.tensor_scalar(xviews[i], xviews[i], 1.0, 0.0, ALU.mult,
                                    ALU.add, accum_out=k.mv[:, i:i + 1])
                v.scalar_tensor_tensor(jk, xviews[i], 1.0, xviews[i],
                                       ALU.mult, ALU.mult,
                                       accum_out=k.mv[:, 32 + i:32 + i + 1])
            v.drain()
            if kind == "ln":
                v.tensor_scalar(sums, sums, 1.0 / width, 0.0, ALU.mult, ALU.add)
                v.drain()
                v.scalar_tensor_tensor(k.colA[:, 0:n], sums, -1.0, sums,
                                       ALU.mult, ALU.mult)
                v.drain()
                v.scalar_tensor_tensor(k.colA[:, 0:n], sqs, 1.0 / width,
                                       k.colA[:, 0:n], ALU.mult, ALU.add)
                v.drain()
                v.tensor_scalar(k.colA[:, 0:n], k.colA[:, 0:n], 1.0, LN_EPS,
                                ALU.mult, ALU.add)
            else:
                v.tensor_scalar(k.colA[:, 0:n], sqs, 1.0 / width, LN_EPS,
                                ALU.mult, ALU.add)
            v.drain()
            vx.inc(v.reciprocal(k.colB[:, 0:n], k.colA[:, 0:n]))
            v.wait_ge(sx.h, sx0 + 1)
            if kind == "ln":
                inst = v.scalar_tensor_tensor(k.colB[:, 0:n], sums, -1.0,
                                              k.colC[:, 0:n], ALU.mult, ALU.mult)
            else:
                inst = v.memset(k.colB[:, 0:n], 0.0)
            vx.inc(inst)

        @blk.scalar
        def _(sc):
            sc.wait_ge(vx.h, vx0 + 1)
            sx.inc(sc.activation(k.colC[:, 0:n], k.colB[:, 0:n], AF.Sqrt))



def emit_l0_attn(k: Kb, W: dict, x_src, res_dst, b: int):
    nc = k.nc
    s = k.s
    x_til = x_src.rearrange("b (i p) c -> b p i c", p=P)
    res_til = res_dst.rearrange("(b i p) c -> b p i c", p=P, b=NBATCH)

    with ExitStack() as ph:
        e = ph.enter_context
        k.pt_ctx(ph)
        ph_mid = ph.enter_context(ExitStack())
        e = ph_mid.enter_context
        hn = e(nc.sbuf_tensor(f"a{b}_hn", [P, 8 * SEQ], BF16))
        hnv = hn.ap().rearrange("p (c t) -> p c t", c=8)
        Abuf = e(nc.sbuf_tensor(f"a{b}_A", [P, 17408], BF16))
        aoff = {}
        o = 0
        for j in range(NT):
            aoff[j] = o
            o += (NT - j) * P

        with ExitStack() as sp:
            xfull = sp.enter_context(nc.sbuf_tensor(f"a{b}_xf", [P, NT * D_MODEL], F32))
            hntm = sp.enter_context(nc.sbuf_tensor(f"a{b}_hntm", [P, NT * D_MODEL], BF16))

            with nc.Block() as blk:
                @blk.sync
                def _(sy):
                    k.dma(sy, xfull[:], x_til[b])
            ldv = s["ld"].v
            xv = [xfull[:, D_MODEL * i:D_MODEL * (i + 1)] for i in range(NT)]
            ln_stats(k, xv, D_MODEL, kind="ln", wait=("ld", ldv))
            k.debug_dump(f"mvA{b}", k.mv[:, 0:64])
            k.debug_dump(f"colA{b}", k.colA[:, 0:16])
            k.debug_dump(f"colB{b}", k.colB[:, 0:16])
            k.debug_dump(f"colC{b}", k.colC[:, 0:16])
            k.debug_dump(f"xfull{b}", xfull[:])
            vx_st = s["vx"].v
            with nc.Block() as blk:
                @blk.scalar
                def _(sc):
                    sc.wait_ge(s["vx"].h, vx_st)
                    for i in range(NT):
                        sc.activation(hntm[:, D_MODEL * i:D_MODEL * (i + 1)],
                                      xv[i], AF.Identity,
                                      bias=k.colB[:, i:i + 1], scale=k.colC[:, i:i + 1])

            jobs = []
            for i in range(NT):
                for half in range(2):
                    srcs = [hntm[:, D_MODEL * i + P * (4 * half + j):
                                 D_MODEL * i + P * (4 * half + j) + P]
                            for j in range(4)]
                    dst = hnv[:, 4 * half:4 * half + 4, P * i:P * (i + 1)]
                    jobs.append((srcs, dst))
            k.debug_dump(f"hntm{b}", hntm[:])
            transpose_pipe(k, jobs)
            k.debug_dump(f"hnfm{b}", hn[:])

        ph_bc = ExitStack()
        e = ph_bc.enter_context
        Kfm = e(nc.sbuf_tensor(f"a{b}_K", [P, NT * SEQ], BF16))
        wbuf = e(nc.sbuf_tensor(f"a{b}_w", [P, 8 * 2048], BF16))
        bcol = e(nc.sbuf_tensor(f"a{b}_bcol", [P, 16], F32))
        qcol = e(nc.sbuf_tensor(f"a{b}_qcol", [P, 16], F32))

        with nc.Block() as blk:
            @blk.gpsimd
            def _(g):
                for c in range(8):
                    k.dma(g, wbuf[:, 2048 * c:2048 * (c + 1)],
                          W["Wk"][P * c:P * (c + 1), :], sem="lg")

            @blk.sync
            def _(sy):
                k.dma(sy, bcol[:], W["bk"].rearrange("(j p) -> p j", p=P), slow=True)
        lgv, ldv = s["lg"].v, s["ld"].v

        def emit_mm_k(t, j, pb):
            if j == 0:
                t.wait_ge(s["lg"].h, lgv)
                t.wait_ge(s["ld"].h, ldv)
            kt, tg = divmod(j, 4)
            for c in range(8):
                inst = t.matmul(pb[:, 0:512],
                                wbuf[:, 2048 * c + P * kt:2048 * c + P * kt + P],
                                hnv[:, c, 512 * tg:512 * (tg + 1)],
                                start=(c == 0), stop=(c == 7))
            return inst

        def emit_ev_k(eng, j, pb):
            kt, tg = divmod(j, 4)
            return eng.activation(Kfm[:, SEQ * kt + 512 * tg:SEQ * kt + 512 * (tg + 1)],
                                  pb[:, 0:512], AF.Identity,
                                  bias=bcol[:, kt:kt + 1], scale=1.0)
        pipe(k, 64, emit_mm_k, emit_ev_k, ev_eng="scalar")
        k.debug_dump(f"Kfm{b}", Kfm[:])

        Qf = e(nc.sbuf_tensor(f"a{b}_Qf", [P, 16 * 512], BF16))
        smx = e(nc.sbuf_tensor(f"a{b}_smx", [P, 400], F32))
        pnb = e(nc.sbuf_tensor(f"a{b}_pnb", [P, P], BF16))

        with nc.Block() as blk:
            @blk.gpsimd
            def _(g):
                for c in range(8):
                    k.dma(g, wbuf[:, 2048 * c:2048 * (c + 1)],
                          W["Wq"][P * c:P * (c + 1), :], sem="lg")

            @blk.sync
            def _(sy):
                k.dma(sy, qcol[:], W["bq"].rearrange("(j p) -> p j", p=P), slow=True)
        lgv, ldv = s["lg"].v, s["ld"].v

        Kv = Kfm.ap().rearrange("p (c t) -> p c t", c=16)
        Qv = Qf.ap().rearrange("p (c t) -> p c t", c=16)
        scale0 = 1.0 / float(np.sqrt(2048.0))

        for g in range(4):
            def emit_mm_q(t, j, pb, g=g):
                if g == 0 and j == 0:
                    t.wait_ge(s["lg"].h, lgv)
                    t.wait_ge(s["ld"].h, ldv)
                for c in range(8):
                    inst = t.matmul(pb[:, 0:512],
                                    wbuf[:, 2048 * c + P * j:2048 * c + P * j + P],
                                    hnv[:, c, 512 * g:512 * (g + 1)],
                                    start=(c == 0), stop=(c == 7))
                return inst

            def emit_ev_q(eng, j, pb):
                return eng.activation(Qf[:, 512 * j:512 * (j + 1)], pb[:, 0:512],
                                      AF.Identity, bias=qcol[:, j:j + 1], scale=1.0)
            pipe(k, 16, emit_mm_q, emit_ev_q, ev_eng="scalar")

            sjobs = []
            for j in range(4 * g):
                sjobs.append((j, (4 * g - j) * P, 512, 0))
            for j in range(4 * g, 4 * g + 3):
                wdt = (4 * g + 3 - j) * P
                sjobs.append((j, P, wdt, (j - 4 * g + 1) * P))

            def emit_mm_s(t, jj, pb, sjobs=sjobs):
                j, acoff, wdt, qoffset = sjobs[jj]
                for c in range(16):
                    inst = t.matmul(pb[:, 0:wdt],
                                    Kv[:, c, P * j:P * (j + 1)],
                                    Qv[:, c, qoffset:qoffset + wdt],
                                    start=(c == 0), stop=(c == 15))
                return inst

            def emit_ev_s(eng, jj, pb, sjobs=sjobs):
                j, acoff, wdt, qoffset = sjobs[jj]
                dst = Abuf[:, aoff[j] + acoff:aoff[j] + acoff + wdt]
                return eng.copy(dst, pb[:, 0:wdt])

            if sjobs:
                pipe(k, len(sjobs), emit_mm_s, emit_ev_s, ev_eng="scalar")

            mm0, ev0, vx0, sx0 = s["mm"].v, s["ev"].v, s["vx"].v, s["sx"].v
            with nc.Block() as blk:
                @blk.tensor
                def _(t):
                    for d in range(4):
                        i = 4 * g + d
                        if d >= 1:
                            t.wait_ge(s["ev"].h, ev0 + 2 * d - 1)
                        for c in range(16):
                            inst = t.matmul(k.pb[4][:, 0:P],
                                            Qv[:, c, P * d:P * (d + 1)],
                                            Kv[:, c, P * i:P * (i + 1)],
                                            start=(c == 0), stop=(c == 15))
                        inst.then_inc(s["mm"].h, 1)
                        t.wait_ge(s["vx"].h, vx0 + 3 * (d + 1))
                        inst = t.transpose(k.pt[0][:, 0:P], pnb[:], k.ident[:])
                        inst.then_inc(s["mm"].h, 1)

                @blk.vector
                def _(v):
                    for d in range(4):
                        i = 4 * g + d
                        v.wait_ge(s["mm"].h, mm0 + 2 * d + 1)
                        if d >= 1:
                            v.wait_ge(s["sx"].h, sx0 + d)
                        inst = v.scalar_tensor_tensor(smx[:, 0:P], k.pb[4][:, 0:P],
                                                      scale0, k.mask[:], ALU.mult, ALU.add)
                        inst.then_inc(s["ev"].h, 1)
                        v.drain()
                        v.tensor_reduce(smx[:, 128:129], smx[:, 0:P], AX.X, ALU.max)
                        v.drain()
                        inst = v.tensor_scalar(smx[:, 129:130], smx[:, 128:129], -1.0,
                                               None, ALU.mult, ALU.bypass)
                        inst.then_inc(s["vx"].h, 1)
                        v.wait_ge(s["sx"].h, sx0 + d + 1)
                        inst = v.reciprocal(smx[:, 131:132], smx[:, 130:131])
                        inst.then_inc(s["vx"].h, 1)
                        v.drain()
                        inst = v.tensor_scalar(pnb[:], smx[:, 256:384], smx[:, 131:132],
                                               None, ALU.mult, ALU.bypass)
                        inst.then_inc(s["vx"].h, 1)
                        v.wait_ge(s["mm"].h, mm0 + 2 * d + 2)
                        inst = v.tensor_copy(Abuf[:, aoff[i]:aoff[i] + P], k.pt[0][:, 0:P])
                        inst.then_inc(s["ev"].h, 1)

                @blk.scalar
                def _(sc):
                    for d in range(4):
                        sc.wait_ge(s["vx"].h, vx0 + 3 * d + 1)
                        inst = sc.activation(smx[:, 256:384], smx[:, 0:P], AF.Exp,
                                             bias=smx[:, 129:130], scale=1.0,
                                             accum_out=smx[:, 130:131])
                        inst.then_inc(s["sx"].h, 1)
            s["mm"].v = mm0 + 8
            s["ev"].v = ev0 + 8
            s["vx"].v = vx0 + 12
            s["sx"].v = sx0 + 4
            if g == 3:
                k.debug_dump(f"Abuf{b}", Abuf[:])
                k.debug_dump(f"Qf{b}", Qf[:])

        ph_bc.close()
        yacc = ph.enter_context(
            nc.sbuf_tensor(f"a{b}_y", [P, 8 * SEQ], BF16, side="right"))
        with ExitStack() as sp:
            e2 = sp.enter_context
            Vtm = e2(nc.sbuf_tensor(f"a{b}_V", [P, NT * 1024], BF16))
            whalf = e2(nc.sbuf_tensor(f"a{b}_wh", [P, 8 * 1024], BF16))
            wg = e2(nc.sbuf_tensor(f"a{b}_wg", [P, 8 * 1024], BF16))
            wo = e2(nc.sbuf_tensor(f"a{b}_wo", [P, 8 * 1024], BF16))
            Gg = e2(nc.sbuf_tensor(f"a{b}_G", [P, 8 * 512], BF16))
            og = e2(nc.sbuf_tensor(f"a{b}_og", [P, 8 * 512], BF16))
            brow = e2(nc.sbuf_tensor(f"a{b}_brow", [1, 1024], F32))
            bvbc = e2(nc.sbuf_tensor(f"a{b}_bvbc", [P, 1024], F32))
            gcol = e2(nc.sbuf_tensor(f"a{b}_gcol", [P, 8], F32))
            ocol = e2(nc.sbuf_tensor(f"a{b}_ocol", [P, 8], F32))
            Vv = Vtm.ap().rearrange("p (c t) -> p c t", c=NT)

            for h in range(2):
                with nc.Block() as blk:
                    @blk.gpsimd
                    def _(g, h=h):
                        for c in range(8):
                            k.dma(g, whalf[:, 1024 * c:1024 * (c + 1)],
                                  W["Wv"][P * c:P * (c + 1), 1024 * h:1024 * (h + 1)],
                                  sem="lg")
                        for c in range(8):
                            k.dma(g, wg[:, 1024 * c:1024 * (c + 1)],
                                  W["Wg"][P * c:P * (c + 1), 1024 * h:1024 * (h + 1)],
                                  sem="lg")
                        for c in range(8):
                            k.dma(g, wo[:, 1024 * c:1024 * (c + 1)],
                                  W["Wo"][1024 * h + P * c:1024 * h + P * (c + 1), :],
                                  sem="lg")

                    @blk.sync
                    def _(sy, h=h):
                        k.dma(sy, brow[:],
                              W["bv"][1024 * h:1024 * (h + 1)].rearrange("(a c) -> a c", a=1))
                        k.dma(sy, gcol[:], W["bg"][1024 * h:1024 * (h + 1)]
                              .rearrange("(j p) -> p j", p=P), slow=True)
                        if h == 0:
                            k.dma(sy, ocol[:], W["bo"].rearrange("(j p) -> p j", p=P), slow=True)
                lgv, ldv = s["lg"].v, s["ld"].v

                def _mm_bv(t, j, pb, ldv=ldv):
                    if j == 0:
                        t.wait_ge(s["ld"].h, ldv)
                    w = min(512, 1024 - 512 * j)
                    return t.matmul(pb[:, 0:w], k.ones_col[:],
                                    brow[:, 512 * j:512 * j + w], start=True, stop=True)

                def _ev_bv(eng, j, pb):
                    w = min(512, 1024 - 512 * j)
                    return eng.tensor_copy(bvbc[:, 512 * j:512 * j + w], pb[:, 0:w])
                pipe(k, 2, _mm_bv, _ev_bv, banks=(4, 5), ev_eng="vector")

                def emit_mm_v(t, j, pb, lgv=lgv):
                    if j == 0:
                        t.wait_ge(s["lg"].h, lgv)
                    tc, vg = divmod(j, 2)
                    for c in range(8):
                        inst = t.matmul(pb[:, 0:512],
                                        hnv[:, c, P * tc:P * (tc + 1)],
                                        whalf[:, 1024 * c + 512 * vg:1024 * c + 512 * (vg + 1)],
                                        start=(c == 0), stop=(c == 7))
                    return inst

                def emit_ev_v(eng, j, pb):
                    tc, vg = divmod(j, 2)
                    return eng.tensor_tensor(Vv[:, tc, 512 * vg:512 * (vg + 1)],
                                             pb[:, 0:512],
                                             bvbc[:, 512 * vg:512 * (vg + 1)], ALU.add)
                pipe(k, NT * 2, emit_mm_v, emit_ev_v, ev_eng="vector")

                for g in range(4):
                    def emit_mm_g(t, j, pb, g=g):
                        for c in range(8):
                            inst = t.matmul(pb[:, 0:512],
                                            wg[:, 1024 * c + P * j:1024 * c + P * j + P],
                                            hnv[:, c, 512 * g:512 * (g + 1)],
                                            start=(c == 0), stop=(c == 7))
                        return inst

                    def emit_ev_g(eng, j, pb):
                        return eng.activation(Gg[:, 512 * j:512 * (j + 1)], pb[:, 0:512],
                                              AF.Sigmoid, bias=gcol[:, j:j + 1], scale=1.0)
                    pipe(k, 8, emit_mm_g, emit_ev_g, ev_eng="scalar")

                    def emit_mm_o(t, jj, pb, g=g):
                        for j in range(4 * g + 4):
                            if j <= 4 * g:
                                wdt, ooff, acol = 512, 0, aoff[j] + (4 * g - j) * P
                            else:
                                wdt, ooff, acol = (4 * g + 4 - j) * P, (j - 4 * g) * P, aoff[j]
                            inst = t.matmul(pb[:, ooff:ooff + wdt],
                                            Vv[:, j, P * jj:P * (jj + 1)],
                                            Abuf[:, acol:acol + wdt],
                                            start=(j == 0), stop=(j == 4 * g + 3))
                        return inst

                    def emit_ev_o(eng, jj, pb):
                        return eng.tensor_tensor(og[:, 512 * jj:512 * (jj + 1)],
                                                 pb[:, 0:512],
                                                 Gg[:, 512 * jj:512 * (jj + 1)], ALU.mult)
                    pipe(k, 8, emit_mm_o, emit_ev_o, ev_eng="vector")

                    def emit_mm_y(t, j, pb):
                        for c in range(8):
                            inst = t.matmul(pb[:, 0:512],
                                            wo[:, 1024 * c + P * j:1024 * c + P * j + P],
                                            og[:, 512 * c:512 * (c + 1)],
                                            start=(c == 0), stop=(c == 7))
                        return inst

                    def emit_ev_y(eng, j, pb, g=g, h=h):
                        dst = yacc[:, SEQ * j + 512 * g:SEQ * j + 512 * (g + 1)]
                        if h == 0:
                            return eng.activation(dst, pb[:, 0:512], AF.Identity,
                                                  bias=ocol[:, j:j + 1], scale=1.0)
                        return eng.tensor_tensor(dst, dst, pb[:, 0:512], ALU.add)
                    pipe(k, 8, emit_mm_y, emit_ev_y,
                         ev_eng=("scalar" if h == 0 else "vector"))

        k.debug_dump(f"yacc{b}", yacc[:])
        ph_mid.close()
        with ExitStack() as sp:
            e2 = sp.enter_context
            ytm = e2(nc.sbuf_tensor(f"a{b}_ytm", [P, NT * D_MODEL], BF16))
            xres = e2(nc.sbuf_tensor(f"a{b}_xr", [P, NT * D_MODEL], F32))
            rest = e2(nc.sbuf_tensor(f"a{b}_rt", [P, NT * D_MODEL], BF16))

            with nc.Block() as blk:
                @blk.sync
                def _(sy):
                    k.dma(sy, xres[:], x_til[b])
            ldv = s["ld"].v

            jobs = []
            for i in range(NT):
                for half in range(2):
                    srcs = [yacc[:, SEQ * (4 * half + j) + P * i:
                                 SEQ * (4 * half + j) + P * i + P] for j in range(4)]
                    dst = ytm[:, D_MODEL * i + 512 * half:D_MODEL * i + 512 * (half + 1)]
                    jobs.append((srcs, dst))
            transpose_pipe(k, jobs)

            yv = [ytm[:, D_MODEL * i:D_MODEL * (i + 1)] for i in range(NT)]
            ln_stats(k, yv, D_MODEL, kind="ln")

            vx0, sx0 = s["vx"].v, s["sx"].v
            with nc.Block() as blk:
                @blk.scalar
                def _(sc):
                    sc.wait_ge(s["ld"].h, ldv)
                    for i in range(NT):
                        inst = sc.activation(xres[:, D_MODEL * i:D_MODEL * (i + 1)],
                                             xres[:, D_MODEL * i:D_MODEL * (i + 1)],
                                             AF.Identity, bias=k.colB[:, i:i + 1],
                                             scale=1.0)
                        inst.then_inc(s["sx"].h, 1)

                @blk.vector
                def _(v):
                    for i in range(NT):
                        v.wait_ge(s["sx"].h, sx0 + i + 1)
                        inst = v.scalar_tensor_tensor(
                            rest[:, D_MODEL * i:D_MODEL * (i + 1)],
                            ytm[:, D_MODEL * i:D_MODEL * (i + 1)],
                            k.colC[:, i:i + 1],
                            xres[:, D_MODEL * i:D_MODEL * (i + 1)],
                            ALU.mult, ALU.add)
                        inst.then_inc(s["vx"].h, 1)
            s["sx"].v = sx0 + NT
            s["vx"].v = vx0 + NT
            vxv = s["vx"].v

            with nc.Block() as blk:
                @blk.sync
                def _(sy):
                    sy.wait_ge(s["vx"].h, vxv)
                    k.dma(sy, res_til[b], rest[:], sem="st")




def emit_l0_ffn(k: Kb, W: dict, res_src, res_dst, tag="f"):
    nc = k.nc
    s = k.s
    FF = 4096
    NW = 2 ** 22
    w1q = nc.dram_tensor(f"sc_{tag}w1q", [4096, 1024], BF16, kind="Internal").ap()
    w2q = nc.dram_tensor(f"sc_{tag}w2q", [1024, 4096], BF16, kind="Internal").ap()
    res_s = res_src.rearrange("(n p) c -> p n c", p=P)
    res_d = res_dst.rearrange("(n p) c -> p n c", p=P)

    def wsrc(t):
        if t < 32:
            return W["w1"][P * t:P * (t + 1), :]
        tt = t - 32
        r, cc = divmod(tt, 4)
        return W["w2"][P * r:P * (r + 1), 1024 * cc:1024 * (cc + 1)]

    def wqdst(t):
        if t < 32:
            return w1q[P * t:P * (t + 1), :]
        tt = t - 32
        r, cc = divmod(tt, 4)
        return w2q[P * r:P * (r + 1), 1024 * cc:1024 * (cc + 1)]

    ph_all = ExitStack()
    mscol = ph_all.enter_context(nc.sbuf_tensor(f"{tag}q_ms", [P, 4], F32,
                                                side="right"))
    with ExitStack() as qp:
        e = qp.enter_context
        wtmp = [e(nc.sbuf_tensor(f"{tag}q_w{i}", [P, 1024], F32)) for i in range(2)]
        tq = [e(nc.sbuf_tensor(f"{tag}q_t{i}", [P, 1024], BF16)) for i in range(2)]
        qa = e(nc.sbuf_tensor(f"{tag}q_qa", [P, 1024], F32))
        acol = e(nc.sbuf_tensor(f"{tag}q_acol", [P, 64], F32))
        tc = e(nc.sbuf_tensor(f"{tag}q_tc", [P, 2], F32))
        tot = e(nc.sbuf_tensor(f"{tag}q_tot", [1, 4], F32))
        ones128 = e(nc.sbuf_tensor(f"{tag}q_o1", [P, 1], F32))

        ld0, sx0 = s["ld"].v, s["sx"].v
        with nc.Block() as blk:
            @blk.sync
            def _(sy):
                for t in range(64):
                    if t >= 2:
                        sy.wait_ge(s["sx"].h, sx0 + t - 1)
                    k.dma(sy, wtmp[t % 2][:], wsrc(t))

            @blk.scalar
            def _(sc):
                for t in range(64):
                    sc.wait_ge(s["ld"].h, ld0 + 16 * (t + 1))
                    inst = sc.activation(k.junk[:, 0:1024], wtmp[t % 2][:], AF.Abs,
                                         accum_out=acol[:, t:t + 1])
                    inst.then_inc(s["sx"].h, 1)
        s["sx"].v = sx0 + 64

        mm0, vx0 = s["mm"].v, s["vx"].v
        with nc.Block() as blk:
            @blk.gpsimd
            def _(g):
                g.memset(ones128[:], 1.0)

            @blk.vector
            def _(v):
                v.tensor_scalar(acol[:, 0:32], acol[:, 0:32], 1.0, 0.0, ALU.mult,
                                ALU.add, accum_out=tc[:, 0:1])
                v.tensor_scalar(acol[:, 32:64], acol[:, 32:64], 1.0, 0.0, ALU.mult,
                                ALU.add, accum_out=tc[:, 1:2])
                inst = v.drain()
                inst.then_inc(s["vx"].h, 1)
                v.wait_ge(s["mm"].h, mm0 + 2)
                v.tensor_copy(tot[0:1, 0:2], k.pb[0][0:1, 0:2])
                v.drain()
                v.tensor_scalar(tot[0:1, 0:2], tot[0:1, 0:2], 1.0 / NW, 1.0,
                                ALU.mult, ALU.mult)
                v.drain()
                v.tensor_scalar(tot[0:1, 0:2], tot[0:1, 0:2], 1e-5, 1.0,
                                ALU.max, ALU.mult)
                v.drain()
                inst = v.reciprocal(tot[0:1, 2:4], tot[0:1, 0:2])
                v.drain()
                inst.then_inc(s["vx"].h, 1)
                v.wait_ge(s["mm"].h, mm0 + 6)
                inst = v.tensor_copy(mscol[:, 0:4], k.pb[1][:, 0:4])
                inst.then_inc(s["vx"].h, 1)

            @blk.tensor
            def _(t):
                t.wait_ge(s["vx"].h, vx0 + 1)
                inst = t.matmul(k.pb[0][0:1, 0:1], tc[:, 0:1], ones128[:, 0:1],
                                start=True, stop=True)
                inst.then_inc(s["mm"].h, 1)
                inst = t.matmul(k.pb[0][0:1, 1:2], tc[:, 1:2], ones128[:, 0:1],
                                start=True, stop=True)
                inst.then_inc(s["mm"].h, 1)
                t.wait_ge(s["vx"].h, vx0 + 2)
                for j, srcap in enumerate([tot[0:1, 0:1], tot[0:1, 1:2],
                                           tot[0:1, 2:3], tot[0:1, 3:4]]):
                    inst = t.matmul(k.pb[1][:, j:j + 1], k.ones_col[:], srcap,
                                    start=True, stop=True)
                    inst.then_inc(s["mm"].h, 1)
        s["mm"].v = mm0 + 6
        s["vx"].v = vx0 + 3

        ld0, vx0, st0 = s["ld"].v, s["vx"].v, s["st"].v
        with nc.Block() as blk:
            @blk.sync
            def _(sy):
                for t in range(64):
                    if t >= 2:
                        sy.wait_ge(s["vx"].h, vx0 + t - 1)
                    k.dma(sy, wtmp[t % 2][:], wsrc(t))
                    if t >= 2:
                        k.dma(sy, wqdst(t - 2), tq[t % 2][:], sem="st")
                for t in (62, 63):
                    sy.wait_ge(s["vx"].h, vx0 + t + 1)
                    k.dma(sy, wqdst(t), tq[t % 2][:], sem="st")

            @blk.vector
            def _(v):
                for t in range(64):
                    c = 0 if t < 32 else 1
                    v.wait_ge(s["ld"].h, ld0 + 16 * (t + 1))
                    if t >= 2:
                        v.wait_ge(s["st"].h, st0 + 16 * (t - 1))
                    v.tensor_scalar(qa[:], wtmp[t % 2][:], mscol[:, 2 + c:3 + c],
                                    MAGIC, ALU.mult, ALU.add)
                    v.drain()
                    v.tensor_scalar(qa[:], qa[:], -MAGIC, -1.0, ALU.add, ALU.max)
                    v.drain()
                    inst = v.tensor_scalar(tq[t % 2][:], qa[:], 1.0, 1.0,
                                           ALU.min, ALU.mult)
                    inst.then_inc(s["vx"].h, 1)
        s["vx"].v = vx0 + 64
    with ph_all as ph:
        e = ph.enter_context
        k.pt_ctx(ph)
        g1b = e(nc.sbuf_tensor(f"{tag}_g1b", [P, 1024], BF16))
        b1lb = e(nc.sbuf_tensor(f"{tag}_b1lb", [P, 1024], BF16))
        g2b = e(nc.sbuf_tensor(f"{tag}_g2b", [P, FF], BF16))
        b2lb = e(nc.sbuf_tensor(f"{tag}_b2lb", [P, FF], BF16))
        b1b = e(nc.sbuf_tensor(f"{tag}_b1b", [P, FF], BF16))
        b2b = e(nc.sbuf_tensor(f"{tag}_b2b", [P, 1024], F32))

        with nc.Block() as blk:
            @blk.gpsimd
            def _(g):
                for name, dst in [("ln1_g", g1b), ("ln1_b", b1lb),
                                  ("ln2_g", g2b), ("ln2_b", b2lb),
                                  ("b1", b1b), ("b2", b2b)]:
                    k.dma(g, dst[:], bc_src(W[name]), sem="lg")
        lgv_bc = s["lg"].v
        with nc.Block() as blk:
            @blk.vector
            def _(v):
                v.wait_ge(s["lg"].h, lgv_bc)

        xg = e(nc.sbuf_tensor(f"{tag}_xg", [P, 2 * 1024], BF16))
        qfm = e(nc.sbuf_tensor(f"{tag}_qfm", [P, 8 * 256], BF16))
        hbuf = e(nc.sbuf_tensor(f"{tag}_h", [P, 2 * FF], BF16))
        q2fm = e(nc.sbuf_tensor(f"{tag}_q2fm", [P, 32 * 256], BF16))
        w1t = [e(nc.sbuf_tensor(f"{tag}_w1t{i}", [P, 8 * 512], BF16)) for i in range(2)]
        w2t = [e(nc.sbuf_tensor(f"{tag}_w2t{i}", [P, 8 * 512], BF16)) for i in range(2)]
        ybuf = e(nc.sbuf_tensor(f"{tag}_y", [P, 2 * 1024], F32))
        uf = [e(nc.sbuf_tensor(f"{tag}_uf{i}", [P, 1024], F32)) for i in range(2)]
        vf = e(nc.sbuf_tensor(f"{tag}_vf", [P, 1024], F32))
        vf2 = e(nc.sbuf_tensor(f"{tag}_vf2", [P, 1024], F32))
        qtm = [e(nc.sbuf_tensor(f"{tag}_qtm{i}", [P, 1024], BF16)) for i in range(2)]
        v2f = e(nc.sbuf_tensor(f"{tag}_v2f", [P, FF], F32))
        v2g = e(nc.sbuf_tensor(f"{tag}_v2g", [P, FF], F32))
        q2tm = e(nc.sbuf_tensor(f"{tag}_q2tm", [P, FF], BF16))
        jw = e(nc.sbuf_tensor(f"{tag}_jw", [P, FF], BF16))
        htmp = [e(nc.sbuf_tensor(f"{tag}_ht{i}", [P, 512], F32)) for i in range(2)]
        oscol = e(nc.sbuf_tensor(f"{tag}_os", [P, 32], F32))
        os2col = e(nc.sbuf_tensor(f"{tag}_os2", [P, 32], F32))
        xaug = e(nc.sbuf_tensor(f"{tag}_xa", [P, 1024], F32))
        resg = e(nc.sbuf_tensor(f"{tag}_rg", [P, 2 * 1024], BF16))

        NTT = 2
        NG = 16

        def quant_tile(v, src_ap, dst_bf_ap, scol_ap, tmp):
            w = src_ap.shape[-1]
            v.tensor_scalar(tmp[:, 0:w], src_ap, scol_ap, MAGIC, ALU.mult, ALU.add)
            v.drain()
            v.tensor_scalar(tmp[:, 0:w], tmp[:, 0:w], -MAGIC, -128.0,
                            ALU.add, ALU.max)
            v.drain()
            return v.tensor_scalar(dst_bf_ap, tmp[:, 0:w], 127.0, 1.0,
                                   ALU.min, ALU.mult)

        for g in range(NG):
            with nc.Block() as blk:
                @blk.sync
                def _(sy, g=g):
                    k.dma(sy, xg.ap().rearrange("p (n c) -> p n c", n=NTT),
                          res_s[:, NTT * g:NTT * (g + 1), :])
            ldv = s["ld"].v
            xtiles = [xg[:, 1024 * i:1024 * (i + 1)] for i in range(NTT)]
            ln_stats(k, xtiles, 1024, kind="ln", wait=("ld", ldv))

            vx0, sx0, mm0, ev0 = s["vx"].v, s["sx"].v, s["mm"].v, s["ev"].v
            with nc.Block() as blk:
                @blk.scalar
                def _(sc, g=g):
                    for i in range(NTT):
                        if i >= 2:
                            sc.wait_ge(s["vx"].h, vx0 + 3 * (i - 2) + 1)
                        inst = sc.activation(uf[i % 2][:], xtiles[i], AF.Identity,
                                             bias=k.colB[:, i:i + 1],
                                             scale=k.colC[:, i:i + 1])
                        inst.then_inc(s["sx"].h, 1)

                @blk.vector
                def _(v, g=g):
                    for i in range(NTT):
                        v.wait_ge(s["sx"].h, sx0 + i + 1)
                        v.scalar_tensor_tensor(vf[:], uf[i % 2][:], 1.0, g1b[:],
                                               ALU.mult, ALU.mult)
                        inst = v.tensor_tensor(vf2[:], vf[:], b1lb[:], ALU.add)
                        inst.then_inc(s["vx"].h, 1)
                        v.tensor_reduce(k.colA[:, 8:9], vf2[:], AX.X, ALU.max,
                                        apply_absolute_value=True)
                        v.drain()
                        v.tensor_scalar(k.colA[:, 9:10], k.colA[:, 8:9], 1e-5, 1.0,
                                        ALU.max, ALU.mult)
                        v.drain()
                        v.reciprocal(k.colA[:, 10:11], k.colA[:, 9:10])
                        v.drain()
                        v.tensor_scalar(k.colA[:, 11:12], k.colA[:, 10:11], 127.0,
                                        0.0, ALU.mult, ALU.add)
                        inst = v.tensor_scalar(oscol[:, NTT * g + i:NTT * g + i + 1],
                                               k.colA[:, 9:10], mscol[:, 0:1],
                                               1.0 / 127.0, ALU.mult, ALU.mult)
                        inst.then_inc(s["vx"].h, 1)
                        v.drain()
                        if i >= 2:
                            v.wait_ge(s["mm"].h, mm0 + 2 * (i - 1))
                        inst = quant_tile(v, vf2[:], qtm[i % 2][:], k.colA[:, 11:12], vf)
                        inst.then_inc(s["vx"].h, 1)
                        if i >= 1:
                            for half in range(2):
                                v.wait_ge(s["mm"].h, mm0 + 2 * (i - 1) + half + 1)
                                dst = qfm.ap().rearrange("p (c t) -> p c t", c=8)[
                                    :, 4 * half:4 * half + 4, P * (i - 1):P * i]
                                inst = v.tensor_copy(dst, k.pt[half][:, 0:512]
                                                     .rearrange("p (c t) -> p c t", c=4))
                                inst.then_inc(s["ev"].h, 1)
                    i = NTT
                    for half in range(2):
                        v.wait_ge(s["mm"].h, mm0 + 2 * (i - 1) + half + 1)
                        dst = qfm.ap().rearrange("p (c t) -> p c t", c=8)[
                            :, 4 * half:4 * half + 4, P * (i - 1):P * i]
                        inst = v.tensor_copy(dst, k.pt[half][:, 0:512]
                                             .rearrange("p (c t) -> p c t", c=4))
                        inst.then_inc(s["ev"].h, 1)

                @blk.tensor
                def _(t, g=g):
                    for i in range(NTT):
                        t.wait_ge(s["vx"].h, vx0 + 3 * i + 3)
                        if i >= 1:
                            t.wait_ge(s["ev"].h, ev0 + 2 * (i - 1) + 2)
                        for half in range(2):
                            for j in range(4):
                                co = P * (4 * half + j)
                                inst = t.transpose(k.pt[half][:, P * j:P * (j + 1)],
                                                   qtm[i % 2][:, co:co + P], k.ident[:])
                            inst.then_inc(s["mm"].h, 1)
            s["sx"].v = sx0 + NTT
            s["vx"].v = vx0 + 3 * NTT
            s["mm"].v = mm0 + 2 * NTT
            s["ev"].v = ev0 + 2 * NTT
            lt0, mm0, ev0, vx0, sx0 = (s["ld"].v, s["mm"].v, s["ev"].v,
                                       s["vx"].v, s["sx"].v)
            qv = qfm.ap().rearrange("p (c t) -> p c t", c=8)
            with nc.Block() as blk:
                @blk.sync
                def _(sy, g=g):
                    for fg in range(8):
                        if fg >= 2:
                            sy.wait_ge(s["mm"].h, mm0 + NTT * (fg - 1))
                        for c in range(8):
                            k.dma(sy, w1t[fg % 2][:, 512 * c:512 * (c + 1)],
                                  w1q[512 * fg:512 * (fg + 1), P * c:P * (c + 1)],
                                  transpose=True)

                @blk.tensor
                def _(t, g=g):
                    for fg in range(8):
                        t.wait_ge(s["ld"].h, lt0 + 16 * 8 * (fg + 1))
                        for tt in range(NTT):
                            j = fg * NTT + tt
                            if j >= 4:
                                t.wait_ge(s["ev"].h, ev0 + j - 3)
                            for c in range(8):
                                inst = t.matmul(k.pb[j % 4][:, 0:512],
                                                qv[:, c, P * tt:P * (tt + 1)],
                                                w1t[fg % 2][:, 512 * c:512 * (c + 1)],
                                                start=(c == 0), stop=(c == 7))
                            inst.then_inc(s["mm"].h, 1)

                @blk.vector
                def _(v, g=g):
                    for j in range(8 * NTT):
                        fg, tt = divmod(j, NTT)
                        v.wait_ge(s["mm"].h, mm0 + j + 1)
                        if j >= 2:
                            v.wait_ge(s["sx"].h, sx0 + j - 1)
                        inst = v.scalar_tensor_tensor(
                            htmp[j % 2][:], k.pb[j % 4][:, 0:512],
                            oscol[:, NTT * g + tt:NTT * g + tt + 1],
                            b1b[:, 512 * fg:512 * (fg + 1)], ALU.mult, ALU.add)
                        inst.then_inc(s["ev"].h, 1)

                @blk.scalar
                def _(sc, g=g):
                    for j in range(8 * NTT):
                        fg, tt = divmod(j, NTT)
                        sc.wait_ge(s["ev"].h, ev0 + j + 1)
                        dst = hbuf[:, FF * tt + 512 * fg:FF * tt + 512 * (fg + 1)]
                        inst = sc.activation(dst, htmp[j % 2][:], AF.Gelu)
                        inst.then_inc(s["sx"].h, 1)
            s["mm"].v = mm0 + 8 * NTT
            s["ev"].v = ev0 + 8 * NTT
            s["sx"].v = sx0 + 8 * NTT
            htiles = [hbuf[:, FF * i:FF * (i + 1)] for i in range(NTT)]
            ln_stats(k, htiles, FF, kind="ln", junk=jw[:])
            vx0, sx0, mm0, ev0 = s["vx"].v, s["sx"].v, s["mm"].v, s["ev"].v
            with nc.Block() as blk:
                @blk.scalar
                def _(sc, g=g):
                    for i in range(NTT):
                        if i >= 1:
                            sc.wait_ge(s["vx"].h, vx0 + 3 * i)
                        inst = sc.activation(v2f[:], htiles[i], AF.Identity,
                                             bias=k.colB[:, i:i + 1],
                                             scale=k.colC[:, i:i + 1])
                        inst.then_inc(s["sx"].h, 1)

                @blk.vector
                def _(v, g=g):
                    for i in range(NTT):
                        v.wait_ge(s["sx"].h, sx0 + i + 1)
                        v.scalar_tensor_tensor(v2g[:], v2f[:], 1.0, g2b[:],
                                               ALU.mult, ALU.mult)
                        inst = v.tensor_tensor(v2f[:], v2g[:], b2lb[:], ALU.add)
                        inst.then_inc(s["vx"].h, 1)
                        v.drain()
                        v.tensor_reduce(k.colA[:, 8:9], v2f[:], AX.X, ALU.max,
                                        apply_absolute_value=True)
                        v.drain()
                        v.tensor_scalar(k.colA[:, 9:10], k.colA[:, 8:9], 1e-5, 1.0,
                                        ALU.max, ALU.mult)
                        v.drain()
                        v.reciprocal(k.colA[:, 10:11], k.colA[:, 9:10])
                        v.drain()
                        v.tensor_scalar(k.colA[:, 11:12], k.colA[:, 10:11], 127.0,
                                        0.0, ALU.mult, ALU.add)
                        inst = v.tensor_scalar(os2col[:, NTT * g + i:NTT * g + i + 1],
                                               k.colA[:, 9:10], mscol[:, 1:2],
                                               1.0 / 127.0, ALU.mult, ALU.mult)
                        inst.then_inc(s["vx"].h, 1)
                        v.drain()
                        if i >= 1:
                            t_prev = 8 * i
                            v.wait_ge(s["mm"].h, mm0 + t_prev)
                        inst = quant_tile(v, v2f[:], q2tm[:], k.colA[:, 11:12], v2g)
                        inst.then_inc(s["vx"].h, 1)
                        for half in range(8):
                            v.wait_ge(s["mm"].h, mm0 + 8 * i + half + 1)
                            dst = q2fm.ap().rearrange("p (c t) -> p c t", c=32)[
                                :, 4 * half:4 * half + 4, P * i:P * (i + 1)]
                            inst = v.tensor_copy(dst, k.pt[half % 2][:, 0:512]
                                                 .rearrange("p (c t) -> p c t", c=4))
                            inst.then_inc(s["ev"].h, 1)

                @blk.tensor
                def _(t, g=g):
                    for i in range(NTT):
                        t.wait_ge(s["vx"].h, vx0 + 3 * i + 3)
                        for half in range(8):
                            if 8 * i + half >= 2:
                                t.wait_ge(s["ev"].h, ev0 + 8 * i + half - 1)
                            for j in range(4):
                                co = P * (4 * half + j)
                                inst = t.transpose(k.pt[half % 2][:, P * j:P * (j + 1)],
                                                   q2tm[:, co:co + P], k.ident[:])
                            inst.then_inc(s["mm"].h, 1)
            s["sx"].v = sx0 + NTT
            s["vx"].v = vx0 + 3 * NTT
            s["mm"].v = mm0 + 8 * NTT
            s["ev"].v = ev0 + 8 * NTT
            lt0, mm0, ev0 = s["ld"].v, s["mm"].v, s["ev"].v
            q2v = q2fm.ap().rearrange("p (c t) -> p c t", c=32)
            with nc.Block() as blk:
                @blk.sync
                def _(sy, g=g):
                    for dg in range(2):
                        for pc in range(4):
                            ip = 4 * dg + pc
                            if ip >= 2:
                                sy.wait_ge(s["mm"].h, mm0 + ip - 1)
                            for c in range(8):
                                cc = 8 * pc + c
                                k.dma(sy, w2t[ip % 2][:, 512 * c:512 * (c + 1)],
                                      w2q[512 * dg:512 * (dg + 1), P * cc:P * (cc + 1)],
                                      transpose=True)

                @blk.tensor
                def _(t, g=g):
                    for dg in range(2):
                        for pc in range(4):
                            ip = 4 * dg + pc
                            t.wait_ge(s["ld"].h, lt0 + 16 * 8 * (ip + 1))
                            if pc == 0 and dg == 1:
                                t.wait_ge(s["ev"].h, ev0 + NTT)
                            for tt in range(NTT):
                                for c in range(8):
                                    cc = 8 * pc + c
                                    inst = t.matmul(
                                        k.pb[tt][:, 0:512],
                                        q2v[:, cc, P * tt:P * (tt + 1)],
                                        w2t[ip % 2][:, 512 * c:512 * (c + 1)],
                                        start=(cc == 0), stop=(cc == 31))
                            inst.then_inc(s["mm"].h, 1)

                @blk.vector
                def _(v, g=g):
                    for dg in range(2):
                        v.wait_ge(s["mm"].h, mm0 + 4 * (dg + 1))
                        for tt in range(NTT):
                            dst = ybuf[:, 1024 * tt + 512 * dg:
                                       1024 * tt + 512 * (dg + 1)]
                            inst = v.scalar_tensor_tensor(
                                dst, k.pb[tt][:, 0:512],
                                os2col[:, NTT * g + tt:NTT * g + tt + 1],
                                b2b[:, 512 * dg:512 * (dg + 1)], ALU.mult, ALU.add)
                            inst.then_inc(s["ev"].h, 1)
            s["mm"].v = mm0 + 8
            s["ev"].v = ev0 + 2 * NTT
            ytiles = [ybuf[:, 1024 * i:1024 * (i + 1)] for i in range(NTT)]
            ln_stats(k, ytiles, 1024, kind="ln")
            vx0, sx0 = s["vx"].v, s["sx"].v
            with nc.Block() as blk:
                @blk.scalar
                def _(sc, g=g):
                    for i in range(NTT):
                        if i >= 1:
                            sc.wait_ge(s["vx"].h, vx0 + i)
                        inst = sc.activation(xaug[:], xtiles[i], AF.Identity,
                                             bias=k.colB[:, i:i + 1], scale=1.0)
                        inst.then_inc(s["sx"].h, 1)

                @blk.vector
                def _(v, g=g):
                    for i in range(NTT):
                        v.wait_ge(s["sx"].h, sx0 + i + 1)
                        inst = v.scalar_tensor_tensor(
                            resg[:, 1024 * i:1024 * (i + 1)], ytiles[i],
                            k.colC[:, i:i + 1], xaug[:], ALU.mult, ALU.add)
                        inst.then_inc(s["vx"].h, 1)

                @blk.sync
                def _(sy, g=g):
                    sy.wait_ge(s["vx"].h, vx0 + NTT)
                    k.dma(sy, res_d[:, NTT * g:NTT * (g + 1), :],
                          resg.ap().rearrange("p (n c) -> p n c", n=NTT), sem="st")
            s["sx"].v = sx0 + NTT
            s["vx"].v = vx0 + NTT


def build_l0ffn_test():
    nc = bass.Bass("TRN2", target_bir_lowering=False)
    xin = nc.dram_tensor("x", [NBATCH * SEQ, D_MODEL], BF16, kind="ExternalInput").ap()
    W = {}
    shp = {"w1": [4096, 1024], "b1": [4096], "ln1_g": [1024], "ln1_b": [1024],
           "ln2_g": [4096], "ln2_b": [4096], "w2": [1024, 4096], "b2": [1024]}
    for kk, sh in shp.items():
        W[kk] = nc.dram_tensor("w_" + kk, sh, F32, kind="ExternalInput").ap()
    res = nc.dram_tensor("res", [NBATCH * SEQ, D_MODEL], BF16,
                         kind="ExternalOutput").ap()
    with ExitStack() as stack:
        k = Kb(nc, stack)
        emit_prelude(k)
        emit_l0_ffn(k, W, xin, res)
        stv = k.s["st"].v
        with nc.Block() as blk:
            @blk.sync
            def _(sy):
                sy.wait_ge(k.s["st"].h, stv)
    return nc




def stage_weight_bf16(k: Kb, pairs, tag):
    nc = k.nc
    s = k.s
    chunks = []
    for (srcw, dstw, rows, cols) in pairs:
        for r in range(rows // P):
            chunks.append((srcw[P * r:P * (r + 1), :], dstw[P * r:P * (r + 1), :],
                           cols))
    with ExitStack() as sp:
        buf = [sp.enter_context(nc.sbuf_tensor(f"{tag}_stg{i}", [P, 2048], BF16))
               for i in range(2)]
        lg0, st0 = s["lg"].v, s["st"].v
        with nc.Block() as blk:
            @blk.gpsimd
            def _(g):
                for i, (sa, da, cols) in enumerate(chunks):
                    if i >= 2:
                        g.wait_ge(s["st"].h, st0 + 16 * (i - 1))
                    k.dma(g, buf[i % 2][:, 0:cols], sa, sem="lg")

            @blk.sync
            def _(sy):
                for i, (sa, da, cols) in enumerate(chunks):
                    sy.wait_ge(s["lg"].h, lg0 + 16 * (i + 1))
                    k.dma(sy, da, buf[i % 2][:, 0:cols], sem="st")
        with nc.Block() as blk:
            @blk.sync
            def _(sy):
                sy.wait_ge(s["st"].h, s["st"].v)


def emit_hn_fm(k: Kb, res_s, b, hnv, tag):
    nc = k.nc
    s = k.s
    with ExitStack() as sp:
        e2 = sp.enter_context
        xfull = e2(nc.sbuf_tensor(f"{tag}_xf", [P, NT * D_MODEL], BF16))
        hntm = e2(nc.sbuf_tensor(f"{tag}_hntm", [P, NT * D_MODEL], BF16))
        with nc.Block() as blk:
            @blk.sync
            def _(sy):
                k.dma(sy, xfull.ap().rearrange("p (n c) -> p n c", n=NT),
                      res_s[:, NT * b:NT * (b + 1), :])
        ldv = s["ld"].v
        xv = [xfull[:, D_MODEL * i:D_MODEL * (i + 1)] for i in range(NT)]
        ln_stats(k, xv, D_MODEL, kind="ln", wait=("ld", ldv))
        vx_st = s["vx"].v
        with nc.Block() as blk:
            @blk.scalar
            def _(sc):
                sc.wait_ge(s["vx"].h, vx_st)
                for i in range(NT):
                    sc.activation(hntm[:, D_MODEL * i:D_MODEL * (i + 1)],
                                  xv[i], AF.Identity,
                                  bias=k.colB[:, i:i + 1], scale=k.colC[:, i:i + 1])
        jobs = []
        for i in range(NT):
            for half in range(2):
                srcs = [hntm[:, D_MODEL * i + P * (4 * half + j):
                             D_MODEL * i + P * (4 * half + j) + P] for j in range(4)]
                dst = hnv[:, 4 * half:4 * half + 4, P * i:P * (i + 1)]
                jobs.append((srcs, dst))
        transpose_pipe(k, jobs)


def emit_l1_attn(k: Kb, W: dict, res_src, res_dst, b: int, scr: dict):
    nc = k.nc
    s = k.s
    DI, DS, TQ = 2048, 16, 512
    res_s = res_src.rearrange("(n p) c -> p n c", p=P)
    res_d = res_dst.rearrange("(n p) c -> p n c", p=P)

    with ExitStack() as ph:
        e = ph.enter_context
        k.pt_ctx(ph)
        u = e(nc.sbuf_tensor(f"m{b}_u", [P, 16 * SEQ], BF16, side="right"))
        uv = u.ap().rearrange("p (c t) -> p c t", c=16)
        cols = e(nc.sbuf_tensor(f"m{b}_cols", [P, 96], F32, side="right"))
        Asb = e(nc.sbuf_tensor(f"m{b}_Asb", [P, 256], F32, side="right"))
        cw = e(nc.sbuf_tensor(f"m{b}_cw", [P, 64], F32, side="right"))
        stq = e(nc.sbuf_tensor(f"m{b}_stq", [P, 256], F32, side="right"))

        with nc.Block() as blk:
            @blk.sync
            def _(sy):
                k.dma(sy, cw.ap().rearrange("p (c t) -> p c t", c=16),
                      W["conv_w"].rearrange("(c p) one t -> p c (one t)", p=P),
                      slow=True)
                k.dma(sy, cols[:, 0:16],
                      W["dt_proj_w"].rearrange("(c p) one -> p c one", p=P)
                      .rearrange("p c one -> p (c one)"), slow=True)
                for nm, o in [("dt_proj_b", 16), ("D_param", 32), ("norm_w", 48),
                              ("norm_b", 64), ("conv_b", 80)]:
                    k.dma(sy, cols[:, o:o + 16],
                          W[nm].rearrange("(c p) -> p c", p=P), slow=True)
                k.dma(sy, Asb.ap().rearrange("p (c n) -> p c n", c=16),
                      W["log_A"].rearrange("(c p) n -> p c n", p=P), slow=True)
        ldv = s["ld"].v
        with nc.Block() as blk:
            @blk.scalar
            def _(sc):
                sc.wait_ge(s["ld"].h, ldv)
                sc.activation(Asb[:], Asb[:], AF.Exp, scale=1.0)
        with nc.Block() as blk:
            @blk.vector
            def _(v):
                v.tensor_scalar(Asb[:], Asb[:], -1.0, 1.0, ALU.mult, ALU.mult)

            @blk.gpsimd
            def _(g):
                g.memset(stq[:], 0.0)

        with ExitStack() as sp:
            e2 = sp.enter_context
            hn = e2(nc.sbuf_tensor(f"m{b}_hn", [P, 8 * SEQ], BF16))
            hnv = hn.ap().rearrange("p (c t) -> p c t", c=8)
            emit_hn_fm(k, res_s, b, hnv, f"m{b}hA")

            xp = e2(nc.sbuf_tensor(f"m{b}_xp", [P, 16 * SEQ], BF16))
            xpv = xp.ap().rearrange("p (c t) -> p c t", c=16)
            sp_wt = ExitStack()
            wt = [sp_wt.enter_context(nc.sbuf_tensor(f"m{b}_wt{i}", [P, 8 * 512],
                                                     BF16)) for i in range(2)]

            lt0, mm0, ev0 = s["ld"].v, s["mm"].v, s["ev"].v
            with nc.Block() as blk:
                @blk.sync
                def _(sy):
                    for xg in range(4):
                        if xg >= 2:
                            sy.wait_ge(s["mm"].h, mm0 + 16 * (xg - 1))
                        for c in range(8):
                            k.dma(sy, wt[xg % 2][:, 512 * c:512 * (c + 1)],
                                  scr["inproj"][512 * xg:512 * (xg + 1),
                                                P * c:P * (c + 1)], transpose=True)

                @blk.tensor
                def _(t):
                    for xg in range(4):
                        t.wait_ge(s["ld"].h, lt0 + 16 * 8 * (xg + 1))
                        for xi in range(4):
                            for tg in range(4):
                                j = 16 * xg + 4 * xi + tg
                                if j >= 4:
                                    t.wait_ge(s["ev"].h, ev0 + j - 3)
                                for c in range(8):
                                    inst = t.matmul(
                                        k.pb[j % 4][:, 0:512],
                                        wt[xg % 2][:, 512 * c + P * xi:
                                                   512 * c + P * (xi + 1)],
                                        hnv[:, c, 512 * tg:512 * (tg + 1)],
                                        start=(c == 0), stop=(c == 7))
                                inst.then_inc(s["mm"].h, 1)

                @blk.vector
                def _(v):
                    for j in range(64):
                        xt, tg = divmod(j, 4)
                        v.wait_ge(s["mm"].h, mm0 + j + 1)
                        inst = v.tensor_copy(xpv[:, xt, 512 * tg:512 * (tg + 1)],
                                             k.pb[j % 4][:, 0:512])
                        inst.then_inc(s["ev"].h, 1)
            s["mm"].v = mm0 + 64
            s["ev"].v = ev0 + 64
            sp_wt.close()
            ca = e2(nc.sbuf_tensor(f"m{b}_ca", [P, SEQ], F32))
            cb = e2(nc.sbuf_tensor(f"m{b}_cb", [P, SEQ], F32))

            sx0, vx0 = s["sx"].v, s["vx"].v
            with nc.Block() as blk:
                @blk.vector
                def _(v):
                    for c in range(16):
                        xc = xpv[:, c, :]
                        v.tensor_scalar(ca[:], xc, cw[:, 4 * c + 3:4 * c + 4], 0.0,
                                        ALU.mult, ALU.add)
                        v.drain()
                        v.scalar_tensor_tensor(cb[:, 1:SEQ], xc[:, 0:SEQ - 1],
                                               cw[:, 4 * c + 2:4 * c + 3],
                                               ca[:, 1:SEQ], ALU.mult, ALU.add)
                        v.tensor_copy(cb[:, 0:1], ca[:, 0:1])
                        v.drain()
                        v.scalar_tensor_tensor(ca[:, 2:SEQ], xc[:, 0:SEQ - 2],
                                               cw[:, 4 * c + 1:4 * c + 2],
                                               cb[:, 2:SEQ], ALU.mult, ALU.add)
                        v.tensor_copy(ca[:, 0:2], cb[:, 0:2])
                        v.drain()
                        v.scalar_tensor_tensor(cb[:, 3:SEQ], xc[:, 0:SEQ - 3],
                                               cw[:, 4 * c + 0:4 * c + 1],
                                               ca[:, 3:SEQ], ALU.mult, ALU.add)
                        v.tensor_copy(cb[:, 0:3], ca[:, 0:3])
                        inst = v.drain()
                        inst.then_inc(s["vx"].h, 1)

                @blk.scalar
                def _(sc):
                    for c in range(16):
                        sc.wait_ge(s["vx"].h, vx0 + c + 1)
                        inst = sc.activation(uv[:, c, :], cb[:], AF.Silu,
                                             bias=cols[:, 80 + c:81 + c], scale=1.0)
                        inst.then_inc(s["sx"].h, 1)
            s["vx"].v = vx0 + 16
            s["sx"].v = sx0 + 16

            xpj = e2(nc.sbuf_tensor(f"m{b}_xpj", [33, 2048], BF16))
            xpjT = e2(nc.sbuf_tensor(f"m{b}_xpjT", [P, 16 * 64], BF16))
            ssm = e2(nc.sbuf_tensor(f"m{b}_ssm", [33, 2048], F32))
            with nc.Block() as blk:
                @blk.gpsimd
                def _(g):
                    k.dma(g, xpj[:], W["x_proj_w"], sem="lg")
            lgv = s["lg"].v
            mm0, ev0, sx0 = s["mm"].v, s["ev"].v, s["sx"].v
            with nc.Block() as blk:
                @blk.tensor
                def _(t):
                    t.wait_ge(s["lg"].h, lgv)
                    for c in range(16):
                        if c >= 2:
                            t.wait_ge(s["ev"].h, ev0 + c - 1)
                        inst = t.transpose(k.pt[c % 2][:, 0:33],
                                           xpj[0:33, P * c:P * (c + 1)],
                                           k.ident[0:33, 0:33])
                        inst.then_inc(s["mm"].h, 1)

                @blk.vector
                def _(v):
                    for c in range(16):
                        v.wait_ge(s["mm"].h, mm0 + c + 1)
                        inst = v.tensor_copy(xpjT[:, 64 * c:64 * c + 33],
                                             k.pt[c % 2][:, 0:33])
                        inst.then_inc(s["ev"].h, 1)
            s["mm"].v = mm0 + 16
            s["ev"].v = ev0 + 16

            mm0, ev0 = s["mm"].v, s["ev"].v
            with nc.Block() as blk:
                @blk.tensor
                def _(t):
                    for tg in range(4):
                        if tg >= 2:
                            t.wait_ge(s["ev"].h, ev0 + tg - 1)
                        for c in range(16):
                            inst = t.matmul(k.pb[tg % 2][0:33, 0:512],
                                            xpjT[:, 64 * c:64 * c + 33],
                                            uv[:, c, 512 * tg:512 * (tg + 1)],
                                            start=(c == 0), stop=(c == 15))
                        inst.then_inc(s["mm"].h, 1)

                @blk.vector
                def _(v):
                    for tg in range(4):
                        v.wait_ge(s["mm"].h, mm0 + tg + 1)
                        inst = v.tensor_copy(ssm[0:33, 512 * tg:512 * (tg + 1)],
                                             k.pb[tg % 2][0:33, 0:512])
                        inst.then_inc(s["ev"].h, 1)
            s["mm"].v = mm0 + 4
            s["ev"].v = ev0 + 4

            with nc.Block() as blk:
                @blk.sync
                def _(sy):
                    k.dma(sy, scr["ssm"][0:33, :], ssm[0:33, :], sem="st")
                    sy.wait_ge(s["st"].h, s["st"].v)

        with ExitStack() as sp:
            e2 = sp.enter_context
            yacc = e2(nc.sbuf_tensor(f"m{b}_yac", [P, 16 * TQ], F32))
            Bb = e2(nc.sbuf_tensor(f"m{b}_Bb", [P, 16 * TQ], BF16))
            Cb = e2(nc.sbuf_tensor(f"m{b}_Cb", [P, 16 * TQ], BF16))
            Xall = e2(nc.sbuf_tensor(f"m{b}_Xa", [P, 16 * TQ], BF16))
            Sall = e2(nc.sbuf_tensor(f"m{b}_Sa", [P, 16 * TQ], BF16))
            Sv = Sall.ap().rearrange("p (n t) -> p n t", n=16)
            drawb = e2(nc.sbuf_tensor(f"m{b}_drw", [P, SEQ], F32))
            delta = e2(nc.sbuf_tensor(f"m{b}_dl", [P, TQ], F32))
            dub = e2(nc.sbuf_tensor(f"m{b}_du", [P, TQ], BF16))
            dApp = [e2(nc.sbuf_tensor(f"m{b}_dA{i}", [P, TQ], BF16)) for i in range(2)]
            cmb = [e2(nc.sbuf_tensor(f"m{b}_cm{i}", [P, TQ], F32)) for i in range(2)]
            sqt = e2(nc.sbuf_tensor(f"m{b}_sq", [P, TQ], F32))
            murow = e2(nc.sbuf_tensor(f"m{b}_mu", [1, 2 * TQ], F32))
            mubc = e2(nc.sbuf_tensor(f"m{b}_mub", [P, TQ], F32))
            rsbc = e2(nc.sbuf_tensor(f"m{b}_rsb", [P, TQ], F32))
            t1b = e2(nc.sbuf_tensor(f"m{b}_t1", [P, TQ], F32))
            t2b = e2(nc.sbuf_tensor(f"m{b}_t2", [P, TQ], F32))
            onesc = e2(nc.sbuf_tensor(f"m{b}_o1", [P, 1], F32))

            with nc.Block() as blk:
                @blk.gpsimd
                def _(g):
                    g.memset(onesc[:], 1.0)
                    k.dma(g, drawb[:], bc_src(scr["ssm"][32]), sem="lg")
            lg_draw = s["lg"].v

            for h in range(4):
                with nc.Block() as blk:
                    @blk.gpsimd
                    def _(g, h=h):
                        for n in range(DS):
                            k.dma(g, Bb[:, TQ * n:TQ * (n + 1)],
                                  bc_src(scr["ssm"][n][TQ * h:TQ * (h + 1)]),
                                  sem="lg")
                            k.dma(g, Cb[:, TQ * n:TQ * (n + 1)],
                                  bc_src(scr["ssm"][16 + n][TQ * h:TQ * (h + 1)]),
                                  sem="lg")
                lgv = s["lg"].v

                vx0, sx0 = s["vx"].v, s["sx"].v
                with nc.Block() as blk:
                    @blk.vector
                    def _(v, h=h, vx0=vx0, sx0=sx0):
                        v.wait_ge(s["lg"].h, lgv)
                        v.memset(yacc[:], 0.0)
                        for c in range(16):
                            svx = vx0 + 18 * c
                            ssx = sx0 + 18 * c
                            v.wait_ge(s["sx"].h, ssx + 1)
                            v.tensor_scalar(delta[:], delta[:], 20.0, 1.0,
                                            ALU.min, ALU.mult)
                            inst = v.drain()
                            inst.then_inc(s["vx"].h, 1)
                            v.wait_ge(s["sx"].h, ssx + 2)
                            v.tensor_scalar(delta[:], delta[:], 0.001, 0.1,
                                            ALU.max, ALU.min)
                            v.drain()
                            inst = v.tensor_tensor(dub[:], delta[:],
                                                   uv[:, c, TQ * h:TQ * (h + 1)],
                                                   ALU.mult)
                            inst.then_inc(s["vx"].h, 1)
                            for n in range(DS):
                                v.tensor_tensor(Xall[:, TQ * n:TQ * (n + 1)], dub[:],
                                                Bb[:, TQ * n:TQ * (n + 1)], ALU.mult)
                            for n in range(DS):
                                v.wait_ge(s["sx"].h, ssx + 2 + n + 1)
                                inst = v.tensor_tensor_scan(
                                    Sv[:, n, :], dApp[n % 2][:],
                                    Xall[:, TQ * n:TQ * (n + 1)],
                                    stq[:, 16 * c + n:16 * c + n + 1],
                                    ALU.mult, ALU.add)
                                inst.then_inc(s["vx"].h, 1)
                            v.tensor_copy(stq[:, 16 * c:16 * (c + 1)],
                                          Sv[:, :, TQ - 1])
                            v.tensor_tensor(cmb[0][:], Sv[:, 0, :],
                                            Cb[:, 0:TQ], ALU.mult)
                            for n in range(1, DS):
                                v.tensor_tensor(cmb[n % 2][:], Sv[:, n, :],
                                                Cb[:, TQ * n:TQ * (n + 1)], ALU.mult)
                                v.tensor_tensor(yacc[:, TQ * c:TQ * (c + 1)],
                                                cmb[(n - 1) % 2][:],
                                                yacc[:, TQ * c:TQ * (c + 1)], ALU.add)
                            v.tensor_tensor(yacc[:, TQ * c:TQ * (c + 1)],
                                            cmb[(DS - 1) % 2][:],
                                            yacc[:, TQ * c:TQ * (c + 1)], ALU.add)

                    @blk.scalar
                    def _(sc, h=h, vx0=vx0, sx0=sx0):
                        for c in range(16):
                            svx = vx0 + 18 * c
                            ssx = sx0 + 18 * c
                            if c >= 1:
                                sc.wait_ge(s["vx"].h, svx - 16)
                            inst = sc.activation(delta[:],
                                                 drawb[:, TQ * h:TQ * (h + 1)],
                                                 AF.Identity,
                                                 scale=cols[:, c:c + 1],
                                                 bias=cols[:, 16 + c:17 + c])
                            inst.then_inc(s["sx"].h, 1)
                            sc.wait_ge(s["vx"].h, svx + 1)
                            sc.activation(delta[:], delta[:], AF.Exp)
                            sc.drain()
                            inst = sc.activation(delta[:], delta[:], AF.Ln, bias=1.0)
                            inst.then_inc(s["sx"].h, 1)
                            sc.wait_ge(s["vx"].h, svx + 2)
                            for n in range(DS):
                                if n >= 2:
                                    sc.wait_ge(s["vx"].h, svx + 2 + n - 1)
                                inst = sc.activation(dApp[n % 2][:], delta[:],
                                                     AF.Exp,
                                                     scale=Asb[:, 16 * c + n:
                                                               16 * c + n + 1])
                                inst.then_inc(s["sx"].h, 1)
                s["vx"].v = vx0 + 18 * 16
                s["sx"].v = sx0 + 18 * 16
                vx0, sx0, mm0 = s["vx"].v, s["sx"].v, s["mm"].v
                with nc.Block() as blk:
                    @blk.vector
                    def _(v, h=h):
                        for c in range(16):
                            inst = v.scalar_tensor_tensor(
                                yacc[:, TQ * c:TQ * (c + 1)],
                                uv[:, c, TQ * h:TQ * (h + 1)],
                                cols[:, 32 + c:33 + c],
                                yacc[:, TQ * c:TQ * (c + 1)], ALU.mult, ALU.add)
                            inst.then_inc(s["vx"].h, 1)

                    @blk.scalar
                    def _(sc, h=h):
                        for c in range(16):
                            sc.wait_ge(s["vx"].h, vx0 + c + 1)
                            if c >= 1:
                                sc.wait_ge(s["mm"].h, mm0 + 2 * c)
                            inst = sc.activation(sqt[:],
                                                 yacc[:, TQ * c:TQ * (c + 1)],
                                                 AF.Square)
                            inst.then_inc(s["sx"].h, 1)

                    @blk.tensor
                    def _(t, h=h):
                        for c in range(16):
                            t.wait_ge(s["vx"].h, vx0 + c + 1)
                            inst = t.matmul(k.pb[4][0:1, 0:TQ], onesc[:],
                                            yacc[:, TQ * c:TQ * (c + 1)],
                                            start=(c == 0), stop=(c == 15))
                            inst.then_inc(s["mm"].h, 1)
                            t.wait_ge(s["sx"].h, sx0 + c + 1)
                            inst = t.matmul(k.pb[5][0:1, 0:TQ], onesc[:], sqt[:],
                                            start=(c == 0), stop=(c == 15))
                            inst.then_inc(s["mm"].h, 1)
                s["vx"].v = vx0 + 16
                s["sx"].v = sx0 + 16
                s["mm"].v = mm0 + 32

                vx0, sx0, mm0 = s["vx"].v, s["sx"].v, s["mm"].v
                with nc.Block() as blk:
                    @blk.vector
                    def _(v, h=h, vx0=vx0, sx0=sx0, mm0=mm0):
                        v.tensor_copy(murow[0:1, 0:TQ], k.pb[4][0:1, 0:TQ])
                        v.tensor_copy(murow[0:1, TQ:2 * TQ], k.pb[5][0:1, 0:TQ])
                        v.drain()
                        v.tensor_scalar(murow[0:1, 0:2 * TQ], murow[0:1, 0:2 * TQ],
                                        1.0 / DI, 0.0, ALU.mult, ALU.add)
                        v.drain()
                        v.scalar_tensor_tensor(sqt[0:1, 0:TQ], murow[0:1, 0:TQ],
                                               -1.0, murow[0:1, 0:TQ],
                                               ALU.mult, ALU.mult)
                        v.drain()
                        v.tensor_tensor(murow[0:1, TQ:2 * TQ], sqt[0:1, 0:TQ],
                                        murow[0:1, TQ:2 * TQ], ALU.add)
                        v.drain()
                        v.tensor_scalar(murow[0:1, TQ:2 * TQ], murow[0:1, TQ:2 * TQ],
                                        1.0, LN_EPS, ALU.mult, ALU.add)
                        v.drain()
                        inst = v.reciprocal(sqt[0:1, 0:TQ], murow[0:1, TQ:2 * TQ])
                        v.drain()
                        inst.then_inc(s["vx"].h, 1)
                        v.wait_ge(s["mm"].h, mm0 + 2)
                        v.tensor_copy(mubc[:], k.pb[4][:, 0:TQ])
                        inst = v.tensor_copy(rsbc[:], k.pb[5][:, 0:TQ])
                        inst.then_inc(s["vx"].h, 1)
                        for c in range(16):
                            if c >= 1:
                                v.wait_ge(s["sx"].h, sx0 + c + 1)
                            v.tensor_tensor(t1b[:], yacc[:, TQ * c:TQ * (c + 1)],
                                            mubc[:], ALU.subtract)
                            inst = v.tensor_tensor(t2b[:], t1b[:], rsbc[:], ALU.mult)
                            inst.then_inc(s["vx"].h, 1)

                    @blk.scalar
                    def _(sc, h=h, vx0=vx0, sx0=sx0):
                        sc.wait_ge(s["vx"].h, vx0 + 1)
                        inst = sc.activation(murow[0:1, TQ:2 * TQ], sqt[0:1, 0:TQ],
                                             AF.Sqrt)
                        inst.then_inc(s["sx"].h, 1)
                        for c in range(16):
                            sc.wait_ge(s["vx"].h, vx0 + 2 + c + 1)
                            inst = sc.activation(uv[:, c, TQ * h:TQ * (h + 1)],
                                                 t2b[:], AF.Identity,
                                                 scale=cols[:, 48 + c:49 + c],
                                                 bias=cols[:, 64 + c:65 + c])
                            inst.then_inc(s["sx"].h, 1)

                    @blk.tensor
                    def _(t, h=h, sx0=sx0):
                        t.wait_ge(s["sx"].h, sx0 + 1)
                        inst = t.matmul(k.pb[4][:, 0:TQ], k.ones_col[:],
                                        murow[0:1, 0:TQ], start=True, stop=True)
                        inst.then_inc(s["mm"].h, 1)
                        inst = t.matmul(k.pb[5][:, 0:TQ], k.ones_col[:],
                                        murow[0:1, TQ:2 * TQ], start=True, stop=True)
                        inst.then_inc(s["mm"].h, 1)
                s["vx"].v = vx0 + 2 + 16
                s["sx"].v = sx0 + 1 + 16
                s["mm"].v = mm0 + 2
        with ExitStack() as sp:
            e2 = sp.enter_context
            sp_z = ExitStack()
            ez = sp_z.enter_context
            hn = ez(nc.sbuf_tensor(f"m{b}_hn2", [P, 8 * SEQ], BF16))
            hnv = hn.ap().rearrange("p (c t) -> p c t", c=8)
            emit_hn_fm(k, res_s, b, hnv, f"m{b}hB")

            wt = [ez(nc.sbuf_tensor(f"m{b}_zw{i}", [P, 8 * 512], BF16))
                  for i in range(2)]
            szb = [ez(nc.sbuf_tensor(f"m{b}_sz{i}", [P, 512], BF16))
                   for i in range(2)]
            lt0, mm0, ev0, vx0 = s["ld"].v, s["mm"].v, s["ev"].v, s["vx"].v
            with nc.Block() as blk:
                @blk.sync
                def _(sy):
                    for zg in range(4):
                        if zg >= 2:
                            sy.wait_ge(s["mm"].h, mm0 + 16 * (zg - 1))
                        for c in range(8):
                            k.dma(sy, wt[zg % 2][:, 512 * c:512 * (c + 1)],
                                  scr["inproj"][2048 + 512 * zg:2048 + 512 * (zg + 1),
                                                P * c:P * (c + 1)], transpose=True)

                @blk.tensor
                def _(t):
                    for zg in range(4):
                        t.wait_ge(s["ld"].h, lt0 + 16 * 8 * (zg + 1))
                        for zi in range(4):
                            for tg in range(4):
                                j = 16 * zg + 4 * zi + tg
                                if j >= 4:
                                    t.wait_ge(s["ev"].h, ev0 + j - 3)
                                for c in range(8):
                                    inst = t.matmul(
                                        k.pb[j % 4][:, 0:512],
                                        wt[zg % 2][:, 512 * c + P * zi:
                                                   512 * c + P * (zi + 1)],
                                        hnv[:, c, 512 * tg:512 * (tg + 1)],
                                        start=(c == 0), stop=(c == 7))
                                inst.then_inc(s["mm"].h, 1)

                @blk.scalar
                def _(sc):
                    for j in range(64):
                        sc.wait_ge(s["mm"].h, mm0 + j + 1)
                        if j >= 2:
                            sc.wait_ge(s["vx"].h, vx0 + j - 1)
                        inst = sc.activation(szb[j % 2][:], k.pb[j % 4][:, 0:512],
                                             AF.Silu)
                        inst.then_inc(s["ev"].h, 1)

                @blk.vector
                def _(v):
                    for j in range(64):
                        zt, tg = divmod(j, 4)
                        v.wait_ge(s["ev"].h, ev0 + j + 1)
                        inst = v.tensor_tensor(uv[:, zt, 512 * tg:512 * (tg + 1)],
                                               uv[:, zt, 512 * tg:512 * (tg + 1)],
                                               szb[j % 2][:], ALU.mult)
                        inst.then_inc(s["vx"].h, 1)
            s["mm"].v = mm0 + 64
            s["ev"].v = ev0 + 64
            s["vx"].v = vx0 + 64
            sp_z.close()

            opT = e2(nc.sbuf_tensor(f"m{b}_opT", [P, 16 * 1024], BF16))
            ytm = e2(nc.sbuf_tensor(f"m{b}_ytm", [P, NT * D_MODEL], BF16))
            lt0 = s["ld"].v
            with nc.Block() as blk:
                @blk.sync
                def _(sy):
                    for ic in range(16):
                        for dg in range(2):
                            k.dma(sy, opT[:, 1024 * ic + 512 * dg:
                                          1024 * ic + 512 * (dg + 1)],
                                  scr["outproj"][512 * dg:512 * (dg + 1),
                                                 P * ic:P * (ic + 1)],
                                  transpose=True)
            mm0, ev0 = s["mm"].v, s["ev"].v
            lt_all = s["ld"].v
            with nc.Block() as blk:
                @blk.tensor
                def _(t):
                    t.wait_ge(s["ld"].h, lt_all)
                    for tt in range(16):
                        for dg in range(2):
                            j = 2 * tt + dg
                            if j >= 4:
                                t.wait_ge(s["ev"].h, ev0 + j - 3)
                            for ic in range(16):
                                inst = t.matmul(
                                    k.pb[j % 4][:, 0:512],
                                    uv[:, ic, P * tt:P * (tt + 1)],
                                    opT[:, 1024 * ic + 512 * dg:
                                        1024 * ic + 512 * (dg + 1)],
                                    start=(ic == 0), stop=(ic == 15))
                            inst.then_inc(s["mm"].h, 1)

                @blk.vector
                def _(v):
                    for j in range(32):
                        tt, dg = divmod(j, 2)
                        v.wait_ge(s["mm"].h, mm0 + j + 1)
                        inst = v.tensor_copy(
                            ytm[:, D_MODEL * tt + 512 * dg:
                                D_MODEL * tt + 512 * (dg + 1)],
                            k.pb[j % 4][:, 0:512])
                        inst.then_inc(s["ev"].h, 1)
            s["mm"].v = mm0 + 32
            s["ev"].v = ev0 + 32
            xres = e2(nc.sbuf_tensor(f"m{b}_xr", [P, NT * D_MODEL], BF16))
            xaug = [e2(nc.sbuf_tensor(f"m{b}_xa{i}", [P, D_MODEL], F32))
                    for i in range(2)]
            with nc.Block() as blk:
                @blk.sync
                def _(sy):
                    k.dma(sy, xres.ap().rearrange("p (n c) -> p n c", n=NT),
                          res_s[:, NT * b:NT * (b + 1), :])
            ldv = s["ld"].v
            yv = [ytm[:, D_MODEL * i:D_MODEL * (i + 1)] for i in range(NT)]
            ln_stats(k, yv, D_MODEL, kind="ln", wait=("ld", ldv))
            vx0, sx0 = s["vx"].v, s["sx"].v
            with nc.Block() as blk:
                @blk.scalar
                def _(sc):
                    for i in range(NT):
                        if i >= 2:
                            sc.wait_ge(s["vx"].h, vx0 + i - 1)
                        inst = sc.activation(xaug[i % 2][:],
                                             xres[:, D_MODEL * i:D_MODEL * (i + 1)],
                                             AF.Identity, bias=k.colB[:, i:i + 1],
                                             scale=1.0)
                        inst.then_inc(s["sx"].h, 1)

                @blk.vector
                def _(v):
                    for i in range(NT):
                        v.wait_ge(s["sx"].h, sx0 + i + 1)
                        inst = v.scalar_tensor_tensor(
                            xres[:, D_MODEL * i:D_MODEL * (i + 1)], yv[i],
                            k.colC[:, i:i + 1], xaug[i % 2][:], ALU.mult, ALU.add)
                        inst.then_inc(s["vx"].h, 1)
            s["sx"].v = sx0 + NT
            s["vx"].v = vx0 + NT
            vxv = s["vx"].v
            with nc.Block() as blk:
                @blk.sync
                def _(sy):
                    sy.wait_ge(s["vx"].h, vxv)
                    k.dma(sy, res_d[:, NT * b:NT * (b + 1), :],
                          xres.ap().rearrange("p (n c) -> p n c", n=NT), sem="st")


L1A_KEYS = ["in_proj_w", "conv_w", "conv_b", "x_proj_w", "dt_proj_w", "dt_proj_b",
            "log_A", "D_param", "norm_w", "norm_b", "out_proj_w"]


def build_l1attn_test():
    nc = bass.Bass("TRN2", target_bir_lowering=False)
    xin = nc.dram_tensor("x", [NBATCH * SEQ, D_MODEL], BF16, kind="ExternalInput").ap()
    shp = {"in_proj_w": [4096, 1024], "conv_w": [2048, 1, 4], "conv_b": [2048],
           "x_proj_w": [33, 2048], "dt_proj_w": [2048, 1], "dt_proj_b": [2048],
           "log_A": [2048, 16], "D_param": [2048], "norm_w": [2048],
           "norm_b": [2048], "out_proj_w": [1024, 2048]}
    W = {kk: nc.dram_tensor("w_" + kk, sh, F32, kind="ExternalInput").ap()
         for kk, sh in shp.items()}
    res = nc.dram_tensor("res", [NBATCH * SEQ, D_MODEL], BF16,
                         kind="ExternalOutput").ap()
    scr = {
        "inproj": nc.dram_tensor("sc_inproj", [4096, 1024], BF16, kind="Internal").ap(),
        "outproj": nc.dram_tensor("sc_outproj", [1024, 2048], BF16, kind="Internal").ap(),
        "ssm": nc.dram_tensor("sc_ssm", [40, 2048], F32, kind="Internal").ap(),
    }
    with ExitStack() as stack:
        k = Kb(nc, stack)
        emit_prelude(k)
        stage_weight_bf16(k, [(W["in_proj_w"], scr["inproj"], 4096, 1024),
                              (W["out_proj_w"], scr["outproj"], 1024, 2048)], "stg")
        for b in range(NBATCH):
            emit_l1_attn(k, W, xin, res, b, scr)
        stv = k.s["st"].v
        with nc.Block() as blk:
            @blk.sync
            def _(sy):
                sy.wait_ge(k.s["st"].h, stv)
    return nc




def emit_qgroup_tiles(k: Kb, v, tiles):
    mx, mn, sA, rA, loA, hiA = (k.colA[:, 20:21], k.colA[:, 21:22],
                                k.colA[:, 22:23], k.colA[:, 23:24],
                                k.colA[:, 24:25], k.colA[:, 25:26])
    for (srca, dsta) in tiles:
        v.tensor_reduce(mx, srca, AX.X, ALU.max)
        v.tensor_reduce(mn, srca, AX.X, ALU.min)
        v.drain()
        v.tensor_tensor(sA, mx, mn, ALU.subtract)
        v.drain()
        v.tensor_scalar(sA, sA, 1e-5, 1.0 / 15.0, ALU.max, ALU.mult)
        v.drain()
        v.reciprocal(rA, sA)
        v.drain()
        v.scalar_tensor_tensor(loA, mn, -1.0, rA, ALU.mult, ALU.mult)
        v.drain()
        v.tensor_scalar(loA, loA, MAGIC, -MAGIC, ALU.add, ALU.add)
        v.drain()
        v.tensor_scalar(loA, loA, 0.0, 15.0, ALU.max, ALU.min)
        v.drain()
        v.tensor_scalar(hiA, loA, -1.0, 15.0, ALU.mult, ALU.add)
        v.tensor_scalar(loA, loA, -1.0, 0.0, ALU.mult, ALU.add)
        v.drain()
        v.tensor_scalar(k.qscr[:, 0:128], srca, rA, MAGIC, ALU.mult, ALU.add)
        v.drain()
        v.tensor_scalar(k.qscr[:, 128:256], k.qscr[:, 0:128], -MAGIC, None,
                        ALU.add, ALU.bypass)
        v.drain()
        v.tensor_scalar(k.qscr[:, 0:128], k.qscr[:, 128:256], loA, hiA,
                        ALU.max, ALU.min)
        v.drain()
        inst = v.tensor_scalar(dsta, k.qscr[:, 0:128], sA, 0.0, ALU.mult, ALU.add)
        v.drain()
    return inst


def emit_rope_tables(k: Kb, costm, sintm, scr_row):
    import math
    nc = k.nc
    s = k.s
    with ExitStack() as sp:
        e2 = sp.enter_context
        invr = e2(nc.sbuf_tensor("rp_inv", [1, 64], F32))
        invb = e2(nc.sbuf_tensor("rp_invb", [P, 64], F32))
        posc = e2(nc.sbuf_tensor("rp_pos", [P, 1], F32))
        fr = e2(nc.sbuf_tensor("rp_fr", [P, 64], F32))
        t1 = e2(nc.sbuf_tensor("rp_t1", [P, 64], F32))
        t2 = e2(nc.sbuf_tensor("rp_t2", [P, 64], F32))
        sins = e2(nc.sbuf_tensor("rp_sins", [P, 64], F32))
        with nc.Block() as blk:
            @blk.gpsimd
            def _(g):
                g.iota(invr[:], [[1, 64]], base=0, channel_multiplier=0,
                       allow_small_or_imprecise_dtypes=True)
                g.iota(posc[:], [[1, 1]], base=0, channel_multiplier=1,
                       allow_small_or_imprecise_dtypes=True)
        with nc.Block() as blk:
            @blk.scalar
            def _(sc):
                sc.activation(invr[:], invr[:], AF.Exp,
                              scale=-math.log(1e6) / 64.0)
        with nc.Block() as blk:
            @blk.sync
            def _(sy):
                k.dma(sy, scr_row, invr[:], sem="st")
                sy.wait_ge(s["st"].h, s["st"].v)
        with nc.Block() as blk:
            @blk.gpsimd
            def _(g):
                k.dma(g, invb[:], bc_src(scr_row[0]), sem="lg")
        lgv = s["lg"].v
        TWO_PI = 2.0 * math.pi
        for i in range(NT):
            for wi, dst in ((0, sintm), (1, costm)):
                off = 0.0 if wi == 0 else math.pi / 2.0
                vx0, sx0 = s["vx"].v, s["sx"].v
                with nc.Block() as blk:
                    @blk.vector
                    def _(v, i=i, off=off):
                        if i == 0 and off == 0.0:
                            v.wait_ge(s["lg"].h, lgv)
                        v.tensor_scalar(t1[:], invb[:], posc[:, 0:1], 0.0,
                                        ALU.mult, ALU.add)
                        v.drain()
                        v.scalar_tensor_tensor(t1[:], invb[:], float(P * i),
                                               t1[:], ALU.mult, ALU.add)
                        v.drain()
                        v.tensor_scalar(t1[:], t1[:], 1.0, off, ALU.mult, ALU.add)
                        v.drain()
                        v.tensor_scalar(t2[:], t1[:], 1.0 / TWO_PI, MAGIC,
                                        ALU.mult, ALU.add)
                        v.drain()
                        v.tensor_scalar(t2[:], t2[:], -MAGIC, -TWO_PI,
                                        ALU.add, ALU.mult)
                        v.drain()
                        inst = v.tensor_tensor(t2[:], t1[:], t2[:], ALU.add)
                        v.drain()
                        inst.then_inc(s["vx"].h, 1)

                    @blk.scalar
                    def _(sc, i=i, dst=dst, vx0=vx0):
                        sc.wait_ge(s["vx"].h, vx0 + 1)
                        sc.activation(sins[:], t2[:], AF.Sin)
                        sc.drain()
                        sc.copy(dst[:, P * i:P * i + 64], sins[:])
                        sc.copy(dst[:, P * i + 64:P * i + 128], sins[:])
                s["vx"].v = vx0 + 1


def emit_l1_ffn_prep(k: Kb, W: dict, bufs: dict, scr: dict):
    nc = k.nc
    s = k.s
    emit_rope_tables(k, bufs["cos"], bufs["sin"], scr["row"])
    jobs = [("w_q", bufs["wqT"], 1024), ("w_k", bufs["wkT"], 256),
            ("w_v", bufs["wvT"], 256), ("w_o", bufs["woT"], 1024)]
    with ExitStack() as sp:
        e2 = sp.enter_context
        k.pt_ctx(sp)
        wnat = e2(nc.sbuf_tensor("p1_wn", [P, 1024], F32))
        wq = e2(nc.sbuf_tensor("p1_wq", [P, 1024], BF16))
        for name, dstT, rows in jobs:
            for r in range(rows // P):
                with nc.Block() as blk:
                    @blk.sync
                    def _(sy, r=r, name=name):
                        k.dma(sy, wnat[:], W[name][P * r:P * (r + 1), :])
                ldv = s["ld"].v
                with nc.Block() as blk:
                    @blk.vector
                    def _(v, r=r):
                        v.wait_ge(s["ld"].h, ldv)
                        emit_qgroup_tiles(k, v, [(wnat[:, P * j:P * (j + 1)],
                                                  wq[:, P * j:P * (j + 1)])
                                                 for j in range(8)])
                mm0, ev0 = s["mm"].v, s["ev"].v
                with nc.Block() as blk:
                    @blk.tensor
                    def _(t, r=r):
                        for half in range(2):
                            for j in range(4):
                                co = P * (4 * half + j)
                                inst = t.transpose(k.pt[half][:, P * j:P * (j + 1)],
                                                   wq[:, co:co + P], k.ident[:])
                            inst.then_inc(s["mm"].h, 1)

                    @blk.vector
                    def _(v, r=r, dstT=dstT, rows=rows, mm0=mm0):
                        for half in range(2):
                            v.wait_ge(s["mm"].h, mm0 + half + 1)
                            dst = dstT.ap().rearrange("p (c t) -> p c t", c=8)[
                                :, 4 * half:4 * half + 4, P * r:P * (r + 1)]
                            inst = v.tensor_copy(dst, k.pt[half][:, 0:512]
                                                 .rearrange("p (c t) -> p c t", c=4))
                            inst.then_inc(s["ev"].h, 1)
                s["mm"].v = mm0 + 2
                s["ev"].v = ev0 + 2


def emit_l1_ffn_attn(k: Kb, W: dict, res_src, h4_dst, b: int, bufs: dict,
                     hid_scr=None, dbg=False):
    nc = k.nc
    s = k.s
    NH = 8
    res_s = res_src.rearrange("(n p) c -> p n c", p=P)
    h4_d = h4_dst.rearrange("(n p) c -> p n c", p=P)
    hid_d = hid_scr.rearrange("(n p) c -> p n c", p=P)
    scale0 = 1.0 / float(np.sqrt(128.0))

    with ExitStack() as ph:
        e = ph.enter_context
        k.pt_ctx(ph)
        hn3t = e(nc.sbuf_tensor(f"f{b}_hnt", [P, NT * D_MODEL], BF16))
        Qfm = e(nc.sbuf_tensor(f"f{b}_Q", [P, 8 * SEQ], BF16))
        Qv = Qfm.ap().rearrange("p (c t) -> p c t", c=8)
        Kfm = e(nc.sbuf_tensor(f"f{b}_K", [P, 2 * SEQ], BF16))
        Kv = Kfm.ap().rearrange("p (c t) -> p c t", c=2)
        Vtm = e(nc.sbuf_tensor(f"f{b}_V", [P, NT * 256], BF16))
        Vv = Vtm.ap().rearrange("p (c t) -> p c t", c=NT)
        Ofm = e(nc.sbuf_tensor(f"f{b}_O", [P, 8 * SEQ], BF16))
        Ov = Ofm.ap().rearrange("p (c t) -> p c t", c=8)

        with ExitStack() as sp:
            e2 = sp.enter_context
            sp_x = ExitStack()
            g1b = sp_x.enter_context(nc.sbuf_tensor(f"f{b}_g1b", [P, 1024], BF16))
            xful = sp_x.enter_context(
                nc.sbuf_tensor(f"f{b}_xf", [P, NT * D_MODEL], BF16))
            with nc.Block() as blk:
                @blk.sync
                def _(sy):
                    k.dma(sy, xful.ap().rearrange("p (n c) -> p n c", n=NT),
                          res_s[:, NT * b:NT * (b + 1), :])

                @blk.gpsimd
                def _(g):
                    k.dma(g, g1b[:], bc_src(W["g1"]), sem="lg")
            ldv, lgv = s["ld"].v, s["lg"].v
            xv = [xful[:, D_MODEL * i:D_MODEL * (i + 1)] for i in range(NT)]
            ln_stats(k, xv, D_MODEL, kind="ln", wait=("ld", ldv))
            sxh = s["sx"].v
            with nc.Block() as blk:
                @blk.scalar
                def _(sc):
                    for i in range(NT):
                        inst = sc.activation(xv[i], xv[i], AF.Identity,
                                             bias=k.colB[:, i:i + 1],
                                             scale=k.colC[:, i:i + 1])
                        inst.then_inc(s["sx"].h, 1)

                @blk.sync
                def _(sy):
                    for i in range(NT):
                        sy.wait_ge(s["sx"].h, sxh + i + 1)
                        k.dma(sy, hid_d[:, NT * b + i, :], xv[i], sem="st")
            s["sx"].v = sxh + NT
            stv_hid = s["st"].v
            ln_stats(k, xv, D_MODEL, kind="rms")
            vx0, sx0 = s["vx"].v, s["sx"].v
            with nc.Block() as blk:
                @blk.scalar
                def _(sc):
                    for i in range(NT):
                        if i >= 1:
                            sc.wait_ge(s["vx"].h, vx0 + i)
                        inst = sc.activation(k.junk[:, 0:1024], xv[i], AF.Identity,
                                             scale=k.colC[:, i:i + 1])
                        inst.then_inc(s["sx"].h, 1)

                @blk.vector
                def _(v):
                    v.wait_ge(s["lg"].h, lgv)
                    for i in range(NT):
                        v.wait_ge(s["sx"].h, sx0 + i + 1)
                        inst = v.tensor_tensor(hn3t[:, D_MODEL * i:D_MODEL * (i + 1)],
                                               k.junk[:, 0:1024], g1b[:], ALU.mult)
                        inst.then_inc(s["vx"].h, 1)
            s["sx"].v = sx0 + NT
            s["vx"].v = vx0 + NT
            sp_x.close()
            if dbg:
                k.debug_dump(f"hn3t{b}", hn3t[:])
            hn3f = e2(nc.sbuf_tensor(f"f{b}_hnf", [P, 8 * SEQ], BF16))
            hv = hn3f.ap().rearrange("p (c t) -> p c t", c=8)
            jobs = []
            for i in range(NT):
                for half in range(2):
                    srcs = [hn3t[:, D_MODEL * i + P * (4 * half + j):
                                 D_MODEL * i + P * (4 * half + j) + P]
                            for j in range(4)]
                    dst = hv[:, 4 * half:4 * half + 4, P * i:P * (i + 1)]
                    jobs.append((srcs, dst))
            transpose_pipe(k, jobs)

            def proj_fm(dstv, wT, nouttile, cw=1024):
                def emm(t, j, pb):
                    ot, tg = divmod(j, 4)
                    for c in range(8):
                        inst = t.matmul(pb[:, 0:512],
                                        wT[:, cw * c + P * ot:cw * c + P * ot + P],
                                        hv[:, c, 512 * tg:512 * (tg + 1)],
                                        start=(c == 0), stop=(c == 7))
                    return inst

                def eev(eng, j, pb):
                    ot, tg = divmod(j, 4)
                    return eng.copy(dstv[:, ot, 512 * tg:512 * (tg + 1)],
                                    pb[:, 0:512])
                pipe(k, nouttile * 4, emm, eev, ev_eng="scalar")

            proj_fm(Qv, bufs["wqT"], 8)
            proj_fm(Kv, bufs["wkT"], 2, cw=256)
            if dbg:
                k.debug_dump(f"Qpre{b}", Qfm[:])
                k.debug_dump(f"Kpre{b}", Kfm[:])

            def emm_v(t, j, pb):
                for c in range(8):
                    inst = t.matmul(pb[:, 0:256],
                                    hv[:, c, P * j:P * (j + 1)],
                                    bufs["wvT"][:, 256 * c:256 * (c + 1)],
                                    start=(c == 0), stop=(c == 7))
                return inst

            def eev_v(eng, j, pb):
                return eng.copy(Vv[:, j, :], pb[:, 0:256])
            pipe(k, NT, emm_v, eev_v, ev_eng="scalar")

        tsc = e(nc.sbuf_tensor(f"f{b}_ts", [P, P], BF16))
        tsd = e(nc.sbuf_tensor(f"f{b}_td", [P, P], BF16))
        tma = e(nc.sbuf_tensor(f"f{b}_tma", [P, P], BF16))
        tmb = e(nc.sbuf_tensor(f"f{b}_tmb", [P, P], BF16))

        def rope_tile(v, src, dst, i):
            cosb = bufs["cos"][:, P * i:P * (i + 1)]
            sinb = bufs["sin"][:, P * i:P * (i + 1)]
            v.tensor_scalar(tsd[:, 0:64], src[:, 64:128], -1.0, 0.0,
                            ALU.mult, ALU.add)
            v.tensor_copy(tsd[:, 64:128], src[:, 0:64])
            v.tensor_tensor(tsc[:], src, cosb, ALU.mult)
            v.drain()
            v.tensor_tensor(tsd[:], tsd[:], sinb, ALU.mult)
            v.drain()
            return v.tensor_tensor(dst, tsc[:], tsd[:], ALU.add)

        def roundtrip(srcv, nchunk, do_quant):
            for c in range(nchunk):
                mm0, vx0 = s["mm"].v, s["vx"].v
                with nc.Block() as blk:
                    @blk.tensor
                    def _(t, c=c, mm0=mm0, vx0=vx0):
                        for i in range(NT):
                            if i >= 1:
                                t.wait_ge(s["vx"].h, vx0 + 2 * i)
                            inst = t.transpose(k.pt[0][:, 0:P],
                                               srcv[:, c, P * i:P * (i + 1)],
                                               k.ident[:])
                            inst.then_inc(s["mm"].h, 1)
                            t.wait_ge(s["vx"].h, vx0 + 2 * i + 1)
                            inst = t.transpose(k.pt[1][:, 0:P], tmb[:], k.ident[:])
                            inst.then_inc(s["mm"].h, 1)

                    @blk.vector
                    def _(v, c=c, mm0=mm0, vx0=vx0):
                        for i in range(NT):
                            v.wait_ge(s["mm"].h, mm0 + 2 * i + 1)
                            v.tensor_copy(tma[:], k.pt[0][:, 0:P])
                            v.drain()
                            if do_quant:
                                emit_qgroup_tiles(k, v, [(tma[:], tma[:])])
                            inst = rope_tile(v, tma[:], tmb[:], i)
                            inst.then_inc(s["vx"].h, 1)
                            v.wait_ge(s["mm"].h, mm0 + 2 * i + 2)
                            inst = v.tensor_copy(srcv[:, c, P * i:P * (i + 1)],
                                                 k.pt[1][:, 0:P])
                            inst.then_inc(s["vx"].h, 1)
                s["mm"].v = mm0 + 2 * NT
                s["vx"].v = vx0 + 2 * NT

        roundtrip(Qv, 8, False)
        roundtrip(Kv, 2, True)
        with nc.Block() as blk:
            @blk.vector
            def _(v):
                for tc in range(NT):
                    for kvh in range(2):
                        sl = Vv[:, tc, P * kvh:P * (kvh + 1)]
                        emit_qgroup_tiles(k, v, [(sl, sl)])
        if dbg:
            k.debug_dump(f"Qfm{b}", Qfm[:])
            k.debug_dump(f"Kfm{b}", Kfm[:])
            k.debug_dump(f"Vtm{b}", Vtm[:])

        Arow = e(nc.sbuf_tensor(f"f{b}_Ar", [P, SEQ], BF16))
        Atr = e(nc.sbuf_tensor(f"f{b}_At", [P, SEQ], BF16))
        sqd = e(nc.sbuf_tensor(f"f{b}_sqd", [P, P], F32))
        for hh in range(NH):
            kvh = hh // 4
            mm0, ev0, vx0, sx0 = s["mm"].v, s["ev"].v, s["vx"].v, s["sx"].v
            mmo, evo, vxo, sxo = [], [], [], []
            a, bb_, c_, d_ = mm0, ev0, vx0, sx0
            for i in range(NT):
                ntile = i // 4 + 1
                mmo.append(a); evo.append(bb_); vxo.append(c_); sxo.append(d_)
                a += ntile + 2 * (i + 1)
                bb_ += (i + 1) + 1
                c_ += 2
                d_ += ntile
            mmE, evE, vxE, sxE = a, bb_, c_, d_
            with nc.Block() as blk:
                @blk.tensor
                def _(t, hh=hh, kvh=kvh):
                    for i in range(NT):
                        ntile = i // 4 + 1
                        wlast = (i + 1) * P - 512 * (ntile - 1)
                        if i >= 1:
                            t.wait_ge(s["sx"].h, sxo[i])
                        for j in range(ntile):
                            w = 512 if j < ntile - 1 else wlast
                            inst = t.matmul(k.pb[j][:, 0:w],
                                            Qv[:, hh, P * i:P * (i + 1)],
                                            Kv[:, kvh, 512 * j:512 * j + w],
                                            start=True, stop=True)
                            inst.then_inc(s["mm"].h, 1)
                        for j in range(i + 1):
                            t.wait_ge(s["vx"].h, vxo[i] + 2)
                            if j >= 2:
                                t.wait_ge(s["ev"].h, evo[i] + j - 1)
                            inst = t.transpose(k.pt[j % 2][:, 0:P],
                                               Arow[:, P * j:P * (j + 1)],
                                               k.ident[:])
                            inst.then_inc(s["mm"].h, 1)
                        for j in range(i + 1):
                            t.wait_ge(s["ev"].h, evo[i] + j + 1)
                            inst = t.matmul(k.pb[5][:, 0:P],
                                            Vv[:, j, P * kvh:P * (kvh + 1)],
                                            Atr[:, P * j:P * (j + 1)],
                                            start=(j == 0), stop=(j == i))
                            inst.then_inc(s["mm"].h, 1)

                @blk.vector
                def _(v, hh=hh):
                    for i in range(NT):
                        ntile = i // 4 + 1
                        wlast = (i + 1) * P - 512 * (ntile - 1)
                        v.wait_ge(s["mm"].h, mmo[i] + ntile)
                        v.scalar_tensor_tensor(
                            k.pb[ntile - 1][:, wlast - P:wlast],
                            k.pb[ntile - 1][:, wlast - P:wlast], 1.0,
                            k.mask[:], ALU.mult, ALU.add)
                        v.drain()
                        for j in range(ntile):
                            w = 512 if j < ntile - 1 else wlast
                            v.tensor_reduce(k.colA[:, 16 + j:17 + j],
                                            k.pb[j][:, 0:w], AX.X, ALU.max)
                        v.drain()
                        v.tensor_reduce(k.colA[:, 20:21], k.colA[:, 16:16 + ntile],
                                        AX.X, ALU.max)
                        v.drain()
                        inst = v.tensor_scalar(k.colA[:, 21:22], k.colA[:, 20:21],
                                               -scale0, 0.0, ALU.mult, ALU.add)
                        v.drain()
                        inst.then_inc(s["vx"].h, 1)
                        v.wait_ge(s["sx"].h, sxo[i] + ntile)
                        v.tensor_scalar(k.colA[:, 16:16 + ntile],
                                        k.colA[:, 16:16 + ntile], 1.0, 0.0,
                                        ALU.mult, ALU.add,
                                        accum_out=k.colA[:, 26:27])
                        v.drain()
                        v.reciprocal(k.colA[:, 27:28], k.colA[:, 26:27])
                        v.drain()
                        inst = v.tensor_scalar(Arow[:, 0:P * (i + 1)],
                                               Arow[:, 0:P * (i + 1)],
                                               k.colA[:, 27:28], 0.0,
                                               ALU.mult, ALU.add)
                        inst.then_inc(s["vx"].h, 1)
                        for j in range(i + 1):
                            v.wait_ge(s["mm"].h, mmo[i] + ntile + j + 1)
                            inst = v.tensor_copy(Atr[:, P * j:P * (j + 1)],
                                                 k.pt[j % 2][:, 0:P])
                            inst.then_inc(s["ev"].h, 1)
                        v.wait_ge(s["mm"].h, mmo[i] + ntile + 2 * (i + 1))
                        inst = v.tensor_copy(Ov[:, hh, P * i:P * (i + 1)],
                                             k.pb[5][:, 0:P])
                        inst.then_inc(s["ev"].h, 1)

                @blk.scalar
                def _(sc, hh=hh):
                    for i in range(NT):
                        ntile = i // 4 + 1
                        wlast = (i + 1) * P - 512 * (ntile - 1)
                        sc.wait_ge(s["vx"].h, vxo[i] + 1)
                        for j in range(ntile):
                            w = 512 if j < ntile - 1 else wlast
                            inst = sc.activation(Arow[:, 512 * j:512 * j + w],
                                                 k.pb[j][:, 0:w], AF.Exp,
                                                 scale=scale0,
                                                 bias=k.colA[:, 21:22],
                                                 accum_out=k.colA[:, 16 + j:17 + j])
                            inst.then_inc(s["sx"].h, 1)
            s["mm"].v = mmE
            s["ev"].v = evE
            s["vx"].v = vxE
            s["sx"].v = sxE

        if dbg:
            k.debug_dump(f"Ofm{b}", Ofm[:])
        h4g = e(nc.sbuf_tensor(f"f{b}_h4", [P, D_MODEL], BF16))
        hjb = [e(nc.sbuf_tensor(f"f{b}_hj{i}", [P, D_MODEL], BF16))
               for i in range(2)]
        mm0, ev0 = s["mm"].v, s["ev"].v
        stv0 = s["st"].v
        lg0 = s["lg"].v
        with nc.Block() as blk:
            @blk.gpsimd
            def _(g):
                g.wait_ge(s["st"].h, stv_hid)
                for tt in range(NT):
                    if tt >= 2:
                        g.wait_ge(s["ev"].h, ev0 + 2 * (tt - 1))
                    k.dma(g, hjb[tt % 2][:], hid_d[:, NT * b + tt, :], sem="lg")

            @blk.tensor
            def _(t):
                for tt in range(NT):
                    for dg in range(2):
                        j = 2 * tt + dg
                        if j >= 4:
                            t.wait_ge(s["ev"].h, ev0 + j - 3)
                        for c in range(8):
                            inst = t.matmul(k.pb[j % 4][:, 0:512],
                                            Ov[:, c, P * tt:P * (tt + 1)],
                                            bufs["woT"][:, 1024 * c + 512 * dg:
                                                        1024 * c + 512 * (dg + 1)],
                                            start=(c == 0), stop=(c == 7))
                        inst.then_inc(s["mm"].h, 1)

            @blk.vector
            def _(v):
                for j in range(2 * NT):
                    tt, dg = divmod(j, 2)
                    v.wait_ge(s["mm"].h, mm0 + j + 1)
                    v.wait_ge(s["lg"].h, lg0 + 16 * (tt + 1))
                    if j >= 2:
                        v.wait_ge(s["st"].h, stv0 + 16 * (j - 1))
                    inst = v.tensor_tensor(
                        h4g[:, 512 * dg:512 * (dg + 1)], k.pb[j % 4][:, 0:512],
                        hjb[tt % 2][:, 512 * dg:512 * (dg + 1)], ALU.add)
                    inst.then_inc(s["ev"].h, 1)

            @blk.sync
            def _(sy):
                for j in range(2 * NT):
                    tt, dg = divmod(j, 2)
                    sy.wait_ge(s["ev"].h, ev0 + j + 1)
                    k.dma(sy, h4_d[:, NT * b + tt, 512 * dg:512 * (dg + 1)],
                          h4g[:, 512 * dg:512 * (dg + 1)], sem="st")
        s["mm"].v = mm0 + 2 * NT
        s["ev"].v = ev0 + 2 * NT
        with nc.Block() as blk:
            @blk.sync
            def _(sy):
                sy.wait_ge(s["st"].h, s["st"].v)




def emit_moe_quant(k: Kb, W: dict, scr: dict):
    nc = k.nc
    s = k.s
    specs = []
    for ename, sname in (("w1", "w1q"), ("w3", "w3q")):
        for ee in range(8):
            for r in range(28):
                specs.append((W[ename][ee, P * r:P * (r + 1), :],
                              scr[sname][ee, P * r:P * (r + 1), :], 1024))
    for ee in range(8):
        for r in range(8):
            specs.append((W["w2"][ee, P * r:P * (r + 1), :],
                          scr["w2q"][ee, P * r:P * (r + 1), :], 3584))

    with ExitStack() as sp:
        e2 = sp.enter_context
        wti = [e2(nc.sbuf_tensor(f"mq_w{i}", [P, 3584], F32)) for i in range(2)]
        qw = e2(nc.sbuf_tensor("mq_qw", [P, 3584], F32))
        qo = [e2(nc.sbuf_tensor(f"mq_qo{i}", [P, 3584], BF16)) for i in range(2)]
        gcA = e2(nc.sbuf_tensor("mq_gA", [P, 32], F32))
        gcB = e2(nc.sbuf_tensor("mq_gB", [P, 32], F32))
        gcS = e2(nc.sbuf_tensor("mq_gS", [P, 32], F32))
        gcR = e2(nc.sbuf_tensor("mq_gR", [P, 32], F32))
        gcL = e2(nc.sbuf_tensor("mq_gL", [P, 32], F32))
        gcH = e2(nc.sbuf_tensor("mq_gH", [P, 32], F32))

        ld0, vx0, st0 = s["ld"].v, s["vx"].v, s["st"].v
        with nc.Block() as blk:
            @blk.sync
            def _(sy):
                for t, (sa, da, wdt) in enumerate(specs):
                    if t >= 2:
                        sy.wait_ge(s["vx"].h, vx0 + t - 1)
                    k.dma(sy, wti[t % 2][:, 0:wdt], sa)
                    if t >= 2:
                        sa2, da2, wdt2 = specs[t - 2]
                        k.dma(sy, da2, qo[t % 2][:, 0:wdt2], sem="st")
                for t in (len(specs) - 2, len(specs) - 1):
                    sy.wait_ge(s["vx"].h, vx0 + t + 1)
                    sa2, da2, wdt2 = specs[t]
                    k.dma(sy, da2, qo[t % 2][:, 0:wdt2], sem="st")

            @blk.vector
            def _(v):
                for t, (sa, da, wdt) in enumerate(specs):
                    G = wdt // P
                    v.wait_ge(s["ld"].h, ld0 + 16 * (t + 1))
                    if t >= 2:
                        v.wait_ge(s["st"].h, st0 + 16 * (t - 1))
                    wv3 = bass.AP(wti[t % 2].ap().tensor, 0,
                                  [[3584, P], [P, G], [1, P]])
                    v.tensor_reduce(gcA[:, 0:G], wv3, AX.X, ALU.max)
                    v.tensor_reduce(gcB[:, 0:G], wv3, AX.X, ALU.min)
                    v.drain()
                    v.tensor_tensor(gcS[:, 0:G], gcA[:, 0:G], gcB[:, 0:G],
                                    ALU.subtract)
                    v.drain()
                    v.tensor_scalar(gcS[:, 0:G], gcS[:, 0:G], 1e-5, 1.0 / 15.0,
                                    ALU.max, ALU.mult)
                    v.drain()
                    v.reciprocal(gcR[:, 0:G], gcS[:, 0:G])
                    v.drain()
                    v.scalar_tensor_tensor(gcL[:, 0:G], gcB[:, 0:G], -1.0,
                                           gcR[:, 0:G], ALU.mult, ALU.mult)
                    v.drain()
                    v.tensor_scalar(gcL[:, 0:G], gcL[:, 0:G], MAGIC, -MAGIC,
                                    ALU.add, ALU.add)
                    v.drain()
                    v.tensor_scalar(gcL[:, 0:G], gcL[:, 0:G], 0.0, 15.0,
                                    ALU.max, ALU.min)
                    v.drain()
                    v.tensor_scalar(gcH[:, 0:G], gcL[:, 0:G], -1.0, 15.0,
                                    ALU.mult, ALU.add)
                    v.tensor_scalar(gcL[:, 0:G], gcL[:, 0:G], -1.0, 0.0,
                                    ALU.mult, ALU.add)
                    v.drain()
                    wa, qb = wti[t % 2], qo[t % 2]
                    for g in range(G):
                        v.tensor_scalar(qw[:, P * g:P * (g + 1)],
                                        wa[:, P * g:P * (g + 1)],
                                        gcR[:, g:g + 1], MAGIC, ALU.mult, ALU.add)
                    for g in range(G):
                        v.tensor_scalar(qw[:, P * g:P * (g + 1)],
                                        qw[:, P * g:P * (g + 1)],
                                        -MAGIC, None, ALU.add, ALU.bypass)
                    for g in range(G):
                        v.tensor_scalar(qw[:, P * g:P * (g + 1)],
                                        qw[:, P * g:P * (g + 1)],
                                        gcL[:, g:g + 1], gcH[:, g:g + 1],
                                        ALU.max, ALU.min)
                    for g in range(G):
                        inst = v.tensor_scalar(qb[:, P * g:P * (g + 1)],
                                               qw[:, P * g:P * (g + 1)],
                                               gcS[:, g:g + 1], 0.0,
                                               ALU.mult, ALU.add)
                    inst.then_inc(s["vx"].h, 1)
        s["vx"].v = vx0 + len(specs)
        with nc.Block() as blk:
            @blk.sync
            def _(sy):
                sy.wait_ge(s["st"].h, s["st"].v)


MOE_NTB = 4
MOE_NE = 8
MOE_SKIP_UP = False
MOE_SKIP_DOWN = False
MOE_SKIP_FINAL = False
MOE_PLAIN_LOADS = False
MOE_LOADS_ONLY = False
MOE_NS = 7


def emit_l1_ffn_moe(k: Kb, W: dict, res_src, h4_src, out_dst, scr: dict):
    nc = k.nc
    s = k.s
    res_s = res_src.rearrange("(n p) c -> p n c", p=P)
    h4_s = h4_src.rearrange("(n p) c -> p n c", p=P)
    out_til = out_dst.rearrange("b (i p) c -> b p i c", p=P)

    with ExitStack() as ph:
        e = ph.enter_context
        g2b = e(nc.sbuf_tensor("moe_g2b", [P, 1024], BF16))
        wgT = e(nc.sbuf_tensor("moe_wgT", [P, 64], BF16))
        wgn = e(nc.sbuf_tensor("moe_wgn", [8, 1024], BF16))
        h4t = e(nc.sbuf_tensor("moe_h4", [P, 8 * 1024], BF16))
        hn4f = e(nc.sbuf_tensor("moe_hnf", [P, 8 * 1024], BF16))
        hnv = hn4f.ap().rearrange("p (c t) -> p c t", c=8)
        moeacc = e(nc.sbuf_tensor("moe_acc", [P, 8 * 1024], F32))
        lgf = e(nc.sbuf_tensor("moe_lgf", [8, 1024], F32))
        ltm = e(nc.sbuf_tensor("moe_ltm", [P, 8], F32))
        rtm = e(nc.sbuf_tensor("moe_rtm", [P, 64], F32))
        j8 = e(nc.sbuf_tensor("moe_j8", [P, 24], F32))
        rwT = e(nc.sbuf_tensor("moe_rwT", [8, 1024], F32))
        xaug = [e(nc.sbuf_tensor(f"moe_xa{i}", [P, 1024], F32)) for i in range(2)]

        with nc.Block() as blk:
            @blk.gpsimd
            def _(g):
                k.dma(g, wgn[:], W["w_gate"], sem="lg")
                k.dma(g, g2b[:], bc_src(W["g2"]), sem="lg")
        lgv = s["lg"].v
        mm0, ev0 = s["mm"].v, s["ev"].v
        with ExitStack() as spg:
            k.pt_ctx(spg)
            with nc.Block() as blk:
                @blk.tensor
                def _(t):
                    t.wait_ge(s["lg"].h, lgv)
                    for c in range(8):
                        inst = t.transpose(k.pt[c % 2][:, 0:8],
                                           wgn[0:8, P * c:P * (c + 1)],
                                           k.ident[0:8, 0:8])
                        inst.then_inc(s["mm"].h, 1)
                        if c >= 1:
                            t.wait_ge(s["ev"].h, ev0 + c)

                @blk.vector
                def _(v):
                    for c in range(8):
                        v.wait_ge(s["mm"].h, mm0 + c + 1)
                        inst = v.tensor_copy(wgT[:, 8 * c:8 * (c + 1)],
                                             k.pt[c % 2][:, 0:8])
                        inst.then_inc(s["ev"].h, 1)
        s["mm"].v = mm0 + 8
        s["ev"].v = ev0 + 8

        for tb in range(MOE_NTB):
            sp_T = ExitStack()
            k.pt_ctx(sp_T)
            with nc.Block() as blk:
                @blk.sync
                def _(sy, tb=tb):
                    k.dma(sy, h4t.ap().rearrange("p (n c) -> p n c", n=8),
                          h4_s[:, 8 * tb:8 * (tb + 1), :])
            ldv = s["ld"].v
            hviews = [h4t[:, 1024 * i:1024 * (i + 1)] for i in range(8)]
            ln_stats(k, hviews, 1024, kind="rms", wait=("ld", ldv))
            vx0, sx0, mm0, ev0 = s["vx"].v, s["sx"].v, s["mm"].v, s["ev"].v
            with nc.Block() as blk:
                @blk.scalar
                def _(sc, tb=tb):
                    for i in range(8):
                        if i >= 1:
                            sc.wait_ge(s["mm"].h, mm0 + 2 * i)
                        inst = sc.activation(k.junk[:, 0:1024], hviews[i],
                                             AF.Identity, scale=k.colC[:, i:i + 1])
                        inst.then_inc(s["sx"].h, 1)

                @blk.vector
                def _(v, tb=tb):
                    for i in range(8):
                        v.wait_ge(s["sx"].h, sx0 + i + 1)
                        inst = v.tensor_tensor(k.junk[:, 0:1024],
                                               k.junk[:, 0:1024], g2b[:], ALU.mult)
                        inst.then_inc(s["vx"].h, 1)
                        for half in range(2):
                            v.wait_ge(s["mm"].h, mm0 + 2 * i + half + 1)
                            dst = hnv[:, 4 * half:4 * half + 4, P * i:P * (i + 1)]
                            inst = v.tensor_copy(dst, k.pt[half][:, 0:512]
                                                 .rearrange("p (c t) -> p c t", c=4))
                            inst.then_inc(s["ev"].h, 1)

                @blk.tensor
                def _(t, tb=tb):
                    for i in range(8):
                        t.wait_ge(s["vx"].h, vx0 + i + 1)
                        if i >= 1:
                            t.wait_ge(s["ev"].h, ev0 + 2 * i)
                        for half in range(2):
                            for j in range(4):
                                co = P * (4 * half + j)
                                inst = t.transpose(k.pt[half][:, P * j:P * (j + 1)],
                                                   k.junk[:, co:co + P], k.ident[:])
                            inst.then_inc(s["mm"].h, 1)
            s["sx"].v = sx0 + 8
            s["vx"].v = vx0 + 8
            s["mm"].v = mm0 + 16
            s["ev"].v = ev0 + 16
            mm0, ev0 = s["mm"].v, s["ev"].v
            with nc.Block() as blk:
                @blk.tensor
                def _(t, tb=tb):
                    for tg in range(2):
                        for c in range(8):
                            inst = t.matmul(k.pb[tg][0:8, 0:512],
                                            wgT[:, 8 * c:8 * (c + 1)],
                                            hnv[:, c, 512 * tg:512 * (tg + 1)],
                                            start=(c == 0), stop=(c == 7))
                        inst.then_inc(s["mm"].h, 1)

                @blk.vector
                def _(v, tb=tb):
                    for tg in range(2):
                        v.wait_ge(s["mm"].h, mm0 + tg + 1)
                        inst = v.tensor_copy(lgf[0:8, 512 * tg:512 * (tg + 1)],
                                             k.pb[tg][0:8, 0:512])
                        inst.then_inc(s["ev"].h, 1)
            s["mm"].v = mm0 + 2
            s["ev"].v = ev0 + 2

            mm0, ev0 = s["mm"].v, s["ev"].v
            with nc.Block() as blk:
                @blk.tensor
                def _(t, tb=tb):
                    for i in range(8):
                        if i >= 2:
                            t.wait_ge(s["ev"].h, ev0 + i - 1)
                        inst = t.transpose(k.pb[2 + i % 2][:, 0:8],
                                           lgf[0:8, P * i:P * (i + 1)],
                                           k.identf[0:8, 0:8])
                        inst.then_inc(s["mm"].h, 1)

                @blk.vector
                def _(v, tb=tb):
                    for i in range(8):
                        v.wait_ge(s["mm"].h, mm0 + i + 1)
                        inst = v.tensor_copy(rtm[:, 8 * i:8 * (i + 1)],
                                             k.pb[2 + i % 2][:, 0:8])
                        inst.then_inc(s["ev"].h, 1)
            s["mm"].v = mm0 + 8
            s["ev"].v = ev0 + 8

            vx0, sx0 = s["vx"].v, s["sx"].v
            with nc.Block() as blk:
                @blk.vector
                def _(v, tb=tb):
                    for i in range(8):
                        lsl = rtm[:, 8 * i:8 * (i + 1)]
                        v.tensor_reduce(k.colA[:, 20:21], lsl, AX.X, ALU.max)
                        v.drain()
                        v.tensor_scalar(j8[:, 0:8], lsl, k.colA[:, 20:21], None,
                                        ALU.is_ge, ALU.bypass)
                        v.drain()
                        v.scalar_tensor_tensor(j8[:, 8:16], j8[:, 0:8], -1e30,
                                               lsl, ALU.mult, ALU.add)
                        v.drain()
                        v.tensor_reduce(k.colA[:, 21:22], j8[:, 8:16], AX.X,
                                        ALU.max)
                        v.drain()
                        inst = v.scalar_tensor_tensor(k.colA[:, 22:23],
                                                      k.colA[:, 20:21], -1.0,
                                                      k.colA[:, 21:22],
                                                      ALU.mult, ALU.add)
                        inst.then_inc(s["vx"].h, 1)
                        v.wait_ge(s["sx"].h, sx0 + i + 1)
                        v.tensor_scalar(k.colA[:, 24:25], k.colA[:, 23:24], -1.0,
                                        1.0, ALU.mult, ALU.add)
                        v.drain()
                        v.tensor_scalar(j8[:, 16:24], j8[:, 8:16],
                                        k.colA[:, 21:22], None,
                                        ALU.is_ge, ALU.bypass)
                        v.tensor_scalar(j8[:, 0:8], j8[:, 0:8], k.colA[:, 24:25],
                                        0.0, ALU.mult, ALU.add)
                        v.drain()
                        inst = v.scalar_tensor_tensor(lsl, j8[:, 16:24],
                                                      k.colA[:, 23:24], j8[:, 0:8],
                                                      ALU.mult, ALU.add)
                        inst.then_inc(s["vx"].h, 1)

                @blk.scalar
                def _(sc, tb=tb):
                    for i in range(8):
                        sc.wait_ge(s["vx"].h, vx0 + 2 * i + 1)
                        inst = sc.activation(k.colA[:, 23:24], k.colA[:, 22:23],
                                             AF.Sigmoid)
                        inst.then_inc(s["sx"].h, 1)
            s["vx"].v = vx0 + 16
            s["sx"].v = sx0 + 8

            mm0, ev0 = s["mm"].v, s["ev"].v
            with nc.Block() as blk:
                @blk.tensor
                def _(t, tb=tb):
                    for i in range(8):
                        if i >= 2:
                            t.wait_ge(s["ev"].h, ev0 + i - 1)
                        inst = t.transpose(k.pb[2 + i % 2][0:8, 0:P],
                                           rtm[:, 8 * i:8 * (i + 1)], k.identf[:])
                        inst.then_inc(s["mm"].h, 1)

                @blk.vector
                def _(v, tb=tb):
                    for i in range(8):
                        v.wait_ge(s["mm"].h, mm0 + i + 1)
                        inst = v.tensor_copy(rwT[0:8, P * i:P * (i + 1)],
                                             k.pb[2 + i % 2][0:8, 0:P])
                        inst.then_inc(s["ev"].h, 1)

                @blk.sync
                def _(sy, tb=tb):
                    sy.wait_ge(s["ev"].h, ev0 + 8)
                    k.dma(sy, scr["rw"][0:8, :], rwT[0:8, :], sem="st")
                    sy.wait_ge(s["st"].h, s["st"].v)
            s["mm"].v = mm0 + 8
            s["ev"].v = ev0 + 8
            sp_T.close()
            with ExitStack() as spA:
                eA = spA.enter_context
                upall = eA(nc.sbuf_tensor(f"moe_up{tb}", [P, 28 * 1024], BF16))
                upv = upall.ap().rearrange("p (c t) -> p c t", c=28)
                w13 = [eA(nc.sbuf_tensor(f"moe_w13{tb}_{i}", [P, 16 * 512], BF16))
                       for i in range(2)]
                w2s = [eA(nc.sbuf_tensor(f"moe_w2{tb}_{i}", [P, 8 * 512], BF16))
                       for i in range(2)]
                s1b = [eA(nc.sbuf_tensor(f"moe_s1{tb}_{i}", [P, 512], BF16))
                       for i in range(2)]
                rwbc = eA(nc.sbuf_tensor(f"moe_rw{tb}", [P, 1024], F32))
                pb6 = eA(nc.psum_tensor(f"moe_p6{tb}", [P, 512], F32))
                pb7 = eA(nc.psum_tensor(f"moe_p7{tb}", [P, 512], F32))
                pbD = k.pb[0:6] + [pb6, pb7]

                vxz = s["vx"].v
                with nc.Block() as blk:
                    @blk.vector
                    def _(v, tb=tb):
                        inst = v.memset(moeacc[:], 0.0)
                        inst.then_inc(s["vx"].h, 1)
                s["vx"].v = vxz + 1

                for ee in range(MOE_NE):
                    lgv = s["lg"].v
                    with nc.Block() as blk:
                        @blk.gpsimd
                        def _(g, ee=ee, tb=tb):
                            k.dma(g, rwbc[:], bc_src(scr["rw"][ee]), sem="lg")
                    lgv_rw = s["lg"].v

                    lt0, mm0, ev0, sx0, vx0 = (s["ld"].v, s["mm"].v, s["ev"].v,
                                               s["sx"].v, s["vx"].v)
                    if MOE_SKIP_UP:
                        with nc.Block() as blk:
                            @blk.gpsimd
                            def _(g):
                                g.memset(upall[:], 0.0)
                    with nc.Block() as blk:
                        @blk.sync
                        def _(sy, ee=ee, tb=tb):
                            if MOE_SKIP_UP:
                                return
                            for ss in range(MOE_NS):
                                if ss >= 2 and not MOE_LOADS_ONLY:
                                    sy.wait_ge(s["mm"].h, mm0 + 16 * (ss - 1))
                                for c in range(8):
                                    k.dma(sy, w13[ss % 2][:, 512 * c:512 * (c + 1)],
                                          scr["w1q"][ee, 0:P, 0:512] if MOE_PLAIN_LOADS else
                                          scr["w1q"][ee, 512 * ss:512 * (ss + 1),
                                                     P * c:P * (c + 1)],
                                          transpose=not MOE_PLAIN_LOADS)
                                for c in range(8):
                                    k.dma(sy, w13[ss % 2][:, 4096 + 512 * c:
                                                          4096 + 512 * (c + 1)],
                                          scr["w3q"][ee, 0:P, 0:512] if MOE_PLAIN_LOADS else
                                          scr["w3q"][ee, 512 * ss:512 * (ss + 1),
                                                     P * c:P * (c + 1)],
                                          transpose=not MOE_PLAIN_LOADS)

                        @blk.tensor
                        def _(t, ee=ee, tb=tb):
                            if MOE_SKIP_UP:
                                return
                            if MOE_LOADS_ONLY:
                                t.wait_ge(s["ld"].h, lt0 + 16 * 16 * MOE_NS)
                                return
                            for ss in range(MOE_NS):
                                t.wait_ge(s["ld"].h, lt0 + 16 * 16 * (ss + 1))
                                for f in range(4):
                                    for tg in range(2):
                                        j = 16 * ss + 4 * (2 * 0 + tg) + f \
                                            if False else 16 * ss + 8 * tg + 2 * f
                                        ba = k.pb[(2 * f + tg) % 4]
                                        bb2 = k.pb[4 + (2 * f + tg) % 2]
                                        jj = 8 * ss + 2 * f + tg
                                        if jj >= 2:
                                            t.wait_ge(s["ev"].h, ev0 + jj - 1)
                                            t.wait_ge(s["sx"].h, sx0 + jj - 1)
                                        for c in range(8):
                                            inst = t.matmul(
                                                ba[:, 0:512],
                                                w13[ss % 2][:, 512 * c + P * f:
                                                            512 * c + P * (f + 1)],
                                                hnv[:, c, 512 * tg:512 * (tg + 1)],
                                                start=(c == 0), stop=(c == 7))
                                        inst.then_inc(s["mm"].h, 1)
                                        for c in range(8):
                                            inst = t.matmul(
                                                bb2[:, 0:512],
                                                w13[ss % 2][:, 4096 + 512 * c + P * f:
                                                            4096 + 512 * c + P * (f + 1)],
                                                hnv[:, c, 512 * tg:512 * (tg + 1)],
                                                start=(c == 0), stop=(c == 7))
                                        inst.then_inc(s["mm"].h, 1)

                        @blk.scalar
                        def _(sc, ee=ee, tb=tb):
                            if MOE_SKIP_UP or MOE_LOADS_ONLY:
                                return
                            for jj in range(8 * MOE_NS):
                                sc.wait_ge(s["mm"].h, mm0 + 2 * jj + 1)
                                if jj >= 2:
                                    sc.wait_ge(s["ev"].h, ev0 + jj - 1)
                                ss, r2 = divmod(jj, 8)
                                f, tg = r2 // 2, r2 % 2
                                ba = k.pb[(2 * f + tg) % 4]
                                inst = sc.activation(s1b[jj % 2][:], ba[:, 0:512],
                                                     AF.Silu)
                                inst.then_inc(s["sx"].h, 1)

                        @blk.vector
                        def _(v, ee=ee, tb=tb):
                            if MOE_SKIP_UP or MOE_LOADS_ONLY:
                                return
                            for jj in range(8 * MOE_NS):
                                ss, r2 = divmod(jj, 8)
                                f, tg = r2 // 2, r2 % 2
                                bb2 = k.pb[4 + (2 * f + tg) % 2]
                                v.wait_ge(s["mm"].h, mm0 + 2 * jj + 2)
                                v.wait_ge(s["sx"].h, sx0 + jj + 1)
                                inst = v.tensor_tensor(
                                    upv[:, 4 * ss + f, 512 * tg:512 * (tg + 1)],
                                    bb2[:, 0:512], s1b[jj % 2][:], ALU.mult)
                                inst.then_inc(s["ev"].h, 1)
                    if not MOE_SKIP_UP and not MOE_LOADS_ONLY:
                        s["mm"].v = mm0 + 16 * MOE_NS
                        s["ev"].v = ev0 + 8 * MOE_NS
                        s["sx"].v = sx0 + 8 * MOE_NS

                    for half in ([] if MOE_SKIP_DOWN else [0, 1]):
                        lt0, mm0, ev0 = s["ld"].v, s["mm"].v, s["ev"].v
                        with nc.Block() as blk:
                            @blk.sync
                            def _(sy, ee=ee, half=half, tb=tb):
                                for ss in range(7):
                                    if ss >= 2:
                                        sy.wait_ge(s["mm"].h, mm0 + (ss - 1))
                                    for cc in range(4):
                                        for dg in range(2):
                                            k.dma(sy, w2s[ss % 2][
                                                :, 512 * (2 * cc + dg):
                                                512 * (2 * cc + dg + 1)],
                                                scr["w2q"][ee, 512 * dg:512 * (dg + 1),
                                                           P * (4 * ss + cc):
                                                           P * (4 * ss + cc + 1)],
                                                transpose=True)

                            @blk.tensor
                            def _(t, ee=ee, half=half, tb=tb):
                                for ss in range(7):
                                    t.wait_ge(s["ld"].h, lt0 + 16 * 8 * (ss + 1))
                                    if ss == 0:
                                        t.wait_ge(s["ev"].h, ev0)
                                    for dt in range(8):
                                        dg, j4 = dt // 4, dt % 4
                                        for cc in range(4):
                                            inst = t.matmul(
                                                pbD[dt][:, 0:512],
                                                w2s[ss % 2][:, 512 * (2 * cc + dg) +
                                                            P * j4:
                                                            512 * (2 * cc + dg) +
                                                            P * (j4 + 1)],
                                                upv[:, 4 * ss + cc,
                                                    512 * half:512 * (half + 1)],
                                                start=(ss == 0 and cc == 0),
                                                stop=(ss == 6 and cc == 3))
                                    inst2 = t.nop()
                                    inst2.then_inc(s["mm"].h, 1)

                            @blk.vector
                            def _(v, ee=ee, half=half, tb=tb):
                                v.wait_ge(s["mm"].h, mm0 + 7)
                                v.wait_ge(s["lg"].h, lgv_rw)
                                for dt in range(8):
                                    v.tensor_tensor(xaug[0][:, 0:512],
                                                    pbD[dt][:, 0:512],
                                                    rwbc[:, 512 * half:
                                                         512 * (half + 1)],
                                                    ALU.mult)
                                    v.drain()
                                    inst = v.tensor_tensor(
                                        moeacc[:, 1024 * dt + 512 * half:
                                               1024 * dt + 512 * (half + 1)],
                                        xaug[0][:, 0:512],
                                        moeacc[:, 1024 * dt + 512 * half:
                                               1024 * dt + 512 * (half + 1)],
                                        ALU.add)
                                    inst.then_inc(s["ev"].h, 1)
                        s["mm"].v = mm0 + 7
                        s["ev"].v = ev0 + 8
            if MOE_SKIP_FINAL:
                with ExitStack() as spB:
                    eB = spB.enter_context
                    outb = eB(nc.sbuf_tensor(f"moe_ob{tb}", [P, 8 * 1024], F32))
                    with nc.Block() as blk:
                        @blk.gpsimd
                        def _(g, tb=tb):
                            g.memset(outb[:], 0.0)
                        @blk.sync
                        def _(sy, tb=tb):
                            bb0 = tb // 2
                            k.dma(sy, out_til[bb0][:, 8 * (tb % 2):8 * (tb % 2) + 8, :],
                                  outb.ap().rearrange("p (n c) -> p n c", n=8),
                                  sem="st")
                continue
            with ExitStack() as spB:
                eB = spB.enter_context
                hmt = eB(nc.sbuf_tensor(f"moe_hmt{tb}", [P, 8 * 1024], F32))
                xres = eB(nc.sbuf_tensor(f"moe_xr{tb}", [P, 8 * 1024], BF16))
                outb = eB(nc.sbuf_tensor(f"moe_ob{tb}", [P, 8 * 1024], F32))
                pb6 = eB(nc.psum_tensor(f"moe_q6{tb}", [P, 512], F32))
                pb7 = eB(nc.psum_tensor(f"moe_q7{tb}", [P, 512], F32))
                pf = [pb6, pb7]

                with nc.Block() as blk:
                    @blk.sync
                    def _(sy, tb=tb):
                        k.dma(sy, xres.ap().rearrange("p (n c) -> p n c", n=8),
                              res_s[:, 8 * tb:8 * (tb + 1), :])
                ldv = s["ld"].v
                mm0, ev0 = s["mm"].v, s["ev"].v
                with nc.Block() as blk:
                    @blk.tensor
                    def _(t, tb=tb):
                        for i in range(8):
                            for half in range(2):
                                hj = 2 * i + half
                                if hj >= 2:
                                    t.wait_ge(s["ev"].h, ev0 + hj - 1)
                                for j in range(4):
                                    dt = 4 * half + j
                                    inst = t.transpose(
                                        pf[hj % 2][:, P * j:P * (j + 1)],
                                        moeacc[:, 1024 * dt + P * i:
                                               1024 * dt + P * (i + 1)],
                                        k.identf[:])
                                inst.then_inc(s["mm"].h, 1)

                    @blk.vector
                    def _(v, tb=tb):
                        for i in range(8):
                            for half in range(2):
                                hj = 2 * i + half
                                v.wait_ge(s["mm"].h, mm0 + hj + 1)
                                inst = v.tensor_tensor(
                                    hmt[:, 1024 * i + 512 * half:
                                        1024 * i + 512 * (half + 1)],
                                    pf[hj % 2][:, 0:512],
                                    h4t[:, 1024 * i + 512 * half:
                                        1024 * i + 512 * (half + 1)], ALU.add)
                                inst.then_inc(s["ev"].h, 1)
                s["mm"].v = mm0 + 16
                s["ev"].v = ev0 + 16

                hmviews = [hmt[:, 1024 * i:1024 * (i + 1)] for i in range(8)]
                ln_stats(k, hmviews, 1024, kind="ln", wait=("ld", ldv))
                vx0, sx0 = s["vx"].v, s["sx"].v
                with nc.Block() as blk:
                    @blk.scalar
                    def _(sc, tb=tb):
                        for i in range(8):
                            if i >= 2:
                                sc.wait_ge(s["vx"].h, vx0 + i - 1)
                            inst = sc.activation(
                                xaug[i % 2][:],
                                xres[:, 1024 * i:1024 * (i + 1)], AF.Identity,
                                bias=k.colB[:, i:i + 1], scale=1.0)
                            inst.then_inc(s["sx"].h, 1)

                    @blk.vector
                    def _(v, tb=tb):
                        for i in range(8):
                            v.wait_ge(s["sx"].h, sx0 + i + 1)
                            inst = v.scalar_tensor_tensor(
                                outb[:, 1024 * i:1024 * (i + 1)], hmviews[i],
                                k.colC[:, i:i + 1], xaug[i % 2][:],
                                ALU.mult, ALU.add)
                            inst.then_inc(s["vx"].h, 1)
                s["sx"].v = sx0 + 8
                s["vx"].v = vx0 + 8
                vxv = s["vx"].v
                with nc.Block() as blk:
                    @blk.sync
                    def _(sy, tb=tb):
                        sy.wait_ge(s["vx"].h, vxv)
                        bb0 = tb // 2
                        k.dma(sy, out_til[bb0][:, 8 * (tb % 2):8 * (tb % 2) + 8, :],
                              outb.ap().rearrange("p (n c) -> p n c", n=8),
                              sem="st")



L0A_KEYS = ["Wq", "bq", "Wk", "bk", "Wv", "bv", "Wg", "bg", "Wo", "bo"]


def build_l0attn_test():
    nc = bass.Bass("TRN2", target_bir_lowering=False)
    x = nc.dram_tensor("x", [NBATCH, SEQ, D_MODEL], F32, kind="ExternalInput").ap()
    W = {}
    shp = {"Wq": [1024, 2048], "Wk": [1024, 2048], "Wv": [1024, 2048],
           "Wg": [1024, 2048], "Wo": [2048, 1024],
           "bq": [2048], "bk": [2048], "bv": [2048], "bg": [2048], "bo": [1024]}
    for kk in L0A_KEYS:
        W[kk] = nc.dram_tensor("w_" + kk, shp[kk], F32, kind="ExternalInput").ap()
    res = nc.dram_tensor("res", [NBATCH * SEQ, D_MODEL], BF16,
                         kind="ExternalOutput").ap()
    with ExitStack() as stack:
        k = Kb(nc, stack)
        emit_prelude(k)
        for b in range(NBATCH):
            emit_l0_attn(k, W, x, res, b)
        stv = k.s["st"].v
        with nc.Block() as blk:
            @blk.sync
            def _(sy):
                sy.wait_ge(k.s["st"].h, stv)
    return nc


L0F_KEYS = ["ln1_g", "ln1_b", "w1", "b1", "ln2_g", "ln2_b", "w2", "b2"]
L1F_KEYS = ["w_q", "w_k", "w_v", "w_o", "g1", "g2", "w_gate", "w1", "w3", "w2"]

INPUT_SHAPES = {
    "L0_attn_Wq": [1024, 2048], "L0_attn_bq": [2048],
    "L0_attn_Wk": [1024, 2048], "L0_attn_bk": [2048],
    "L0_attn_Wv": [1024, 2048], "L0_attn_bv": [2048],
    "L0_attn_Wg": [1024, 2048], "L0_attn_bg": [2048],
    "L0_attn_Wo": [2048, 1024], "L0_attn_bo": [1024],
    "L0_ffn_ln1_g": [1024], "L0_ffn_ln1_b": [1024],
    "L0_ffn_w1": [4096, 1024], "L0_ffn_b1": [4096],
    "L0_ffn_ln2_g": [4096], "L0_ffn_ln2_b": [4096],
    "L0_ffn_w2": [1024, 4096], "L0_ffn_b2": [1024],
    "L1_attn_in_proj_w": [4096, 1024], "L1_attn_conv_w": [2048, 1, 4],
    "L1_attn_conv_b": [2048], "L1_attn_x_proj_w": [33, 2048],
    "L1_attn_dt_proj_w": [2048, 1], "L1_attn_dt_proj_b": [2048],
    "L1_attn_log_A": [2048, 16], "L1_attn_D_param": [2048],
    "L1_attn_norm_w": [2048], "L1_attn_norm_b": [2048],
    "L1_attn_out_proj_w": [1024, 2048],
    "L1_ffn_w_q": [1024, 1024], "L1_ffn_w_k": [256, 1024],
    "L1_ffn_w_v": [256, 1024], "L1_ffn_w_o": [1024, 1024],
    "L1_ffn_g1": [1024], "L1_ffn_g2": [1024], "L1_ffn_w_gate": [8, 1024],
    "L1_ffn_w1": [8, 3584, 1024], "L1_ffn_w3": [8, 3584, 1024],
    "L1_ffn_w2": [8, 1024, 3584],
}


def build_megakernel():
    nc = bass.Bass("TRN2", target_bir_lowering=False)
    x = nc.dram_tensor("L0_attn_hidden_states", [NBATCH, SEQ, D_MODEL], F32,
                       kind="ExternalInput").ap()
    Win = {nm: nc.dram_tensor(nm, sh, F32, kind="ExternalInput").ap()
           for nm, sh in INPUT_SHAPES.items()}
    W0a = {kk: Win["L0_attn_" + kk] for kk in L0A_KEYS}
    W0f = {kk: Win["L0_ffn_" + kk] for kk in L0F_KEYS}
    W1a = {kk: Win["L1_attn_" + kk] for kk in L1A_KEYS}
    W1f = {kk: Win["L1_ffn_" + kk] for kk in L1F_KEYS}
    out = nc.dram_tensor("out", [NBATCH, SEQ, D_MODEL], F32,
                         kind="ExternalOutput").ap()
    res1 = nc.dram_tensor("sc_res1", [NBATCH * SEQ, D_MODEL], BF16,
                          kind="Internal").ap()
    res2 = nc.dram_tensor("sc_res2", [NBATCH * SEQ, D_MODEL], BF16,
                          kind="Internal").ap()
    res3 = nc.dram_tensor("sc_res3", [NBATCH * SEQ, D_MODEL], BF16,
                          kind="Internal").ap()
    h4 = nc.dram_tensor("sc_h4", [NBATCH * SEQ, D_MODEL], BF16,
                        kind="Internal").ap()
    hid = nc.dram_tensor("sc_hid", [NBATCH * SEQ, D_MODEL], BF16,
                         kind="Internal").ap()
    scr = {
        "inproj": nc.dram_tensor("sc_inproj", [4096, 1024], BF16,
                                 kind="Internal").ap(),
        "outproj": nc.dram_tensor("sc_outproj", [1024, 2048], BF16,
                                  kind="Internal").ap(),
        "ssm": nc.dram_tensor("sc_ssm", [40, 2048], F32, kind="Internal").ap(),
        "w1q": nc.dram_tensor("sc_w1q", [8, 3584, 1024], BF16,
                              kind="Internal").ap(),
        "w3q": nc.dram_tensor("sc_w3q", [8, 3584, 1024], BF16,
                              kind="Internal").ap(),
        "w2q": nc.dram_tensor("sc_w2q", [8, 1024, 3584], BF16,
                              kind="Internal").ap(),
        "rw": nc.dram_tensor("sc_rw", [8, 1024], F32, kind="Internal").ap(),
        "row": nc.dram_tensor("sc_row", [1, 64], F32, kind="Internal").ap(),
    }
    with ExitStack() as stack:
        k = Kb(nc, stack)
        emit_prelude(k)
        for b in range(NBATCH):
            emit_l0_attn(k, W0a, x, res1, b)
        emit_l0_ffn(k, W0f, res1, res2)
        stage_weight_bf16(k, [(W1a["in_proj_w"], scr["inproj"], 4096, 1024),
                              (W1a["out_proj_w"], scr["outproj"], 1024, 2048)],
                          "stg")
        for b in range(NBATCH):
            emit_l1_attn(k, W1a, res2, res3, b, scr)
        with ExitStack() as ffn_ph:
            e = ffn_ph.enter_context
            bufs = {
                "cos": e(nc.sbuf_tensor("b_cos", [P, 16 * 128], BF16,
                                        side="right")),
                "sin": e(nc.sbuf_tensor("b_sin", [P, 16 * 128], BF16,
                                        side="right")),
                "wqT": e(nc.sbuf_tensor("b_wqT", [P, 8 * 1024], BF16,
                                        side="right")),
                "wkT": e(nc.sbuf_tensor("b_wkT", [P, 8 * 256], BF16,
                                        side="right")),
                "wvT": e(nc.sbuf_tensor("b_wvT", [P, 8 * 256], BF16,
                                        side="right")),
                "woT": e(nc.sbuf_tensor("b_woT", [P, 8 * 1024], BF16,
                                        side="right")),
            }
            emit_l1_ffn_prep(k, W1f, bufs, scr)
            for b in range(NBATCH):
                emit_l1_ffn_attn(k, W1f, res3, h4, b, bufs, hid_scr=hid)
        emit_moe_quant(k, W1f, scr)
        emit_l1_ffn_moe(k, W1f, res3, h4, out, scr)
        stv = k.s["st"].v
        with nc.Block() as blk:
            @blk.sync
            def _(sy):
                sy.wait_ge(k.s["st"].h, stv)
    return nc


_NC_CACHE = []


def kernel(**inputs):
    if not _NC_CACHE:
        _NC_CACHE.append(build_megakernel())
    nc = _NC_CACHE[0]
    x = np.ascontiguousarray(np.asarray(inputs["L0_attn_hidden_states"],
                                        dtype=np.float32))
    in_maps = []
    for c in range(8):
        m = {"L0_attn_hidden_states": x[NBATCH * c:NBATCH * (c + 1)]}
        for nm in INPUT_SHAPES:
            m[nm] = np.ascontiguousarray(np.asarray(inputs[nm],
                                                    dtype=np.float32))
        in_maps.append(m)
    res = run_bass_kernel_spmd(nc, in_maps, core_ids=list(range(8)))
    out = np.concatenate([np.asarray(res.results[c]["out"]) for c in range(8)],
                         axis=0)
    return out.reshape(16, SEQ, D_MODEL)
